# version 18
# baseline (speedup 1.0000x reference)
import contextlib
import numpy as np
import ml_dtypes
import concourse.bass as bass
import concourse.mybir as mybir
from concourse.bass import IndirectOffsetOnAxis
from concourse.bass_utils import run_bass_kernel_spmd

F32 = mybir.dt.float32
BF16 = mybir.dt.bfloat16
U32 = mybir.dt.uint32
I32 = mybir.dt.int32
OP = mybir.AluOpType
AF = mybir.ActivationFunctionType
AX = mybir.AxisListType

NCORES = 8
B = 32
SEQ = 2048
D = 1024
I_FF = 2816
NH = 16
HD = 64
TOPU = 409
U = 512
SPC = B // NCORES
TG = 512
NTG = SEQ // TG
RNE1 = 12582912.0
RNE2 = 25165824.0
RNE4 = 50331648.0
BISECT_ITERS = 26


def _gf_pos2g():
    pos2g = -np.ones(3072, dtype=np.int64)
    for c in range(16):
        for p in range(128):
            pos2g[c * 128 + p] = 16 * p + c
    for jj in range(8):
        c = 16 + jj
        for p in range(48):
            pos2g[c * 128 + p] = 16 * (128 + p) + 2 * jj
        for p in range(64, 112):
            pos2g[c * 128 + p] = 16 * (128 + p - 64) + 2 * jj + 1
    return pos2g


class Buf:
    __slots__ = ("name", "last_w", "readers", "preds")

    def __init__(self, name=""):
        self.name = name
        self.last_w = None
        self.readers = []
        self.preds = []


class Node:
    __slots__ = ("stream", "fn", "marked", "seq", "is_dma", "deps",
                 "small", "drain_before", "dma_slot", "dma_round")

    def __init__(self, stream, fn, is_dma, small=False):
        self.stream = stream
        self.fn = fn
        self.is_dma = is_dma
        self.marked = False
        self.seq = -1
        self.deps = []
        self.small = small
        self.drain_before = False
        self.dma_slot = -1
        self.dma_round = 0


STREAMS = ("sync", "vector", "scalar", "tensor", "gpsimd")


class Builder:
    def __init__(self):
        self.by_stream = {s: [] for s in STREAMS}
        self.last_drain = {s: -1 for s in STREAMS}

    def op(self, stream, fn, reads=(), writes=(), is_dma=False, small=False):
        n = Node(stream, fn, is_dma, small=small)
        n.seq = len(self.by_stream[stream])
        self.by_stream[stream].append(n)

        def dep_on(p):
            if p is None or p is n:
                return
            if p.stream == stream and not p.is_dma and not n.is_dma:
                if p.small and p.seq >= self.last_drain[stream]:
                    n.drain_before = True
                    self.last_drain[stream] = n.seq
                return
            n.deps.append(p)

        for b in reads:
            for pb in b.preds:
                dep_on(pb.last_w)
            dep_on(b.last_w)
        for b in writes:
            for pb in b.preds:
                dep_on(pb.last_w)
                for r in pb.readers:
                    dep_on(r)
            dep_on(b.last_w)
            for r in b.readers:
                dep_on(r)
        for b in reads:
            b.readers.append(n)
        for b in writes:
            b.last_w = n
            b.readers = []
            b.preds = []
        return n

    def finalize(self, nc, ctx, block):
        for s in STREAMS:
            for n in self.by_stream[s]:
                if n.is_dma:
                    n.marked = True
                for d in n.deps:
                    d.marked = True
        NPOOL = {"sync": 30, "gpsimd": 16, "scalar": 8, "vector": 8,
                 "tensor": 8}
        cum = {}
        sem_c, sem_p = {}, {}
        for s in STREAMS:
            c = 0
            cc = []
            nd = 0
            for n in self.by_stream[s]:
                if n.is_dma:
                    n.dma_slot = nd % NPOOL[s]
                    n.dma_round = nd // NPOOL[s] + 1
                    nd += 1
                elif n.marked:
                    c += 1
                cc.append(c)
            cum[s] = cc
            if c:
                sem_c[s] = ctx.enter_context(nc.semaphore(f"sc_{s}"))
            if nd:
                sem_p[s] = [ctx.enter_context(nc.semaphore(f"sp_{s}{i}"))
                            for i in range(min(nd, NPOOL[s]))]

        def mk(s):
            def run(eng):
                hw = {}
                for n in self.by_stream[s]:
                    if n.drain_before:
                        eng.drain()
                    if n.is_dma and n.dma_round > 1:
                        key = (s, "d", n.dma_slot)
                        val = 16 * (n.dma_round - 1)
                        if hw.get(key, 0) < val:
                            hw[key] = val
                            eng.wait_ge(sem_p[s][n.dma_slot], val)
                    for dep in n.deps:
                        ds = dep.stream
                        if dep.is_dma:
                            key = (ds, "d", dep.dma_slot)
                            val = 16 * dep.dma_round
                            sem = sem_p[ds][dep.dma_slot]
                        else:
                            key, val = (ds, 0), cum[ds][dep.seq]
                            sem = sem_c[ds]
                        if hw.get(key, 0) >= val:
                            continue
                        hw[key] = val
                        eng.wait_ge(sem, val)
                    inst = n.fn(eng)
                    if n.is_dma:
                        inst.then_inc(sem_p[s][n.dma_slot], 16)
                    elif n.marked:
                        assert inst is not None, \
                            f"marked node on {s} seq {n.seq} returned None"
                        inst.then_inc(sem_c[s], 1)
            return run

        for s in STREAMS:
            if self.by_stream[s]:
                getattr(block, s)(mk(s))


class Arena:
    def __init__(self, nc, name, nbytes):
        self.nc, self.name, self.nbytes = nc, name, nbytes
        self.off = 0
        self.live = []
        self.dead = []

    def materialize(self, ctx):
        self.t = ctx.enter_context(
            self.nc.sbuf_tensor(self.name, [128, self.nbytes],
                                mybir.dt.uint8))

    def alloc(self, shape, dtype, tag=""):
        esz = mybir.dt.size(dtype)
        fre = int(np.prod(shape[1:]))
        start = (self.off + 31) // 32 * 32
        end = start + fre * esz
        assert end <= self.nbytes, \
            f"arena {self.name} overflow {end} > {self.nbytes} at {tag}"
        self.off = end
        buf = Buf(f"{self.name}:{tag}")
        buf.preds = [b for (s0, e0, b) in self.live + self.dead
                     if not (e0 <= start or s0 >= end)]
        self.live.append((start, end, buf))
        self.dead = [(s0, e0, b) for (s0, e0, b) in self.dead
                     if not (start <= s0 and e0 <= end)]
        ap = self.t[:, start:end].bitcast(dtype)
        if len(shape) > 2:
            dims = " ".join(f"d{i}" for i in range(1, len(shape)))
            kw = {f"d{i}": shape[i] for i in range(2, len(shape))}
            ap = ap.rearrange(f"p ({dims}) -> p {dims}", **kw)
        if shape[0] != 128:
            ap = ap[0:shape[0]]
        return ap, buf

    def mark(self):
        return self.off

    def reset(self, mark):
        self.off = mark
        self.dead += [(s, e, b) for (s, e, b) in self.live if e > mark]
        self.live = [(s, e, b) for (s, e, b) in self.live if e <= mark]


def build_program():
    nc = bass.Bass("TRN2", target_bir_lowering=False)

    def din(name, shape, dtype):
        return nc.dram_tensor(name, shape, dtype, kind="ExternalInput").ap()

    def dint(name, shape, dtype):
        return nc.dram_tensor(name, shape, dtype, kind="Internal").ap()

    x_in = din("x", [SPC * SEQ, D], F32)
    wqt_i = din("wqt", [D, D], F32)
    wkt_i = din("wkt", [D, D], F32)
    wvtp_i = din("wvtp", [D, NH * 72], F32)
    wot_i = din("wot", [D, D], F32)
    bq_i = din("bq", [1, D], F32)
    bkr_i = din("bkr", [1, D], F32)
    bvp_i = din("bvp", [1, NH * 72], F32)
    bo_i = din("bo", [1, D], F32)
    wg_i = din("wg", [3072, D], F32)
    wu_i = din("wu", [3072, D], F32)
    wd_i = din("wd", [D, I_FF], F32)
    idb_i = din("idb", [128, 128], BF16)
    idf_i = din("idf", [128, 128], F32)
    iou_i = din("iou", [128, U], F32)
    out_d = nc.dram_tensor("out", [SPC * SEQ, D], F32,
                           kind="ExternalOutput").ap()

    wgq_d = dint("wgq_d", [24, 128, 8, 128], BF16)
    wuq_d = dint("wuq_d", [24, 128, 8, 128], BF16)
    wdq_d = dint("wdq_d", [24, 128, D], BF16)
    x_d = dint("x_d", [SEQ, D], F32)
    q_d = dint("q_d", [SEQ, D], BF16)
    k_d = dint("k_d", [NH, HD, SEQ], BF16)
    of_d = dint("of_d", [SEQ, D], BF16)

    wgq_db = [Buf(f"wgq{m}") for m in range(24)]
    wuq_db = [Buf(f"wuq{m}") for m in range(24)]
    wdq_db = [Buf(f"wdq{m}") for m in range(24)]
    x_db = [Buf(f"x_d{i}") for i in range(16)]
    q_db = [Buf(f"q_d{i}") for i in range(16)]
    k_db = [Buf(f"k_d{h}") for h in range(NH)]
    of_db = [Buf(f"of{h}") for h in range(NH)]
    of_zb = [Buf(f"ofz{i}") for i in range(16)]
    out_db = [Buf(f"out{i}") for i in range(SPC * 16)]

    bld = Builder()
    ctx = contextlib.ExitStack()

    def sbuf(name, shape, dtype):
        ap = ctx.enter_context(nc.sbuf_tensor(name, shape, dtype))
        return ap, Buf(name)

    idb, idb_b = sbuf("idb_s", [128, 128], BF16)
    idf, idf_b = sbuf("idf_s", [128, 128], F32)
    iou, iou_b = sbuf("iou_s", [128, U], F32)
    bq_s, bq_b = sbuf("bq_s", [1, D], BF16)
    bkr_s, bkr_b = sbuf("bkr_s", [1, D], BF16)
    bvp_s, bvp_b = sbuf("bvp_s", [1, NH * 72], BF16)
    bo_s, bo_b = sbuf("bo_s", [1, D], BF16)
    ones1b, ones1b_b = sbuf("ones1b", [1, 512], BF16)
    onescb, onescb_b = sbuf("onescb", [128, 1], BF16)
    zero_t, zero_b = sbuf("zero_t", [128, D], BF16)
    eps_s, eps_b = sbuf("eps_s", [128, 1], F32)

    AR = Arena(nc, "arena", 170 * 1024)
    AR.materialize(ctx)

    PS = []
    for i in range(8):
        ap = ctx.enter_context(nc.psum_tensor(f"ps{i}", [128, 512], F32))
        PS.append((ap, Buf(f"ps{i}")))

    def psf(i):
        return PS[i]

    def psb(i):
        ap, b = PS[i]
        return ap[:].bitcast(BF16), b

    def O(stream, fn, reads=(), writes=(), dma=False, small=False):
        return bld.op(stream, fn, reads=reads, writes=writes, is_dma=dma,
                      small=small)

    def TS(out, outb, in0, in0b, s1, op0, s2=None, op1=None, accum_out=None,
           accumb=None, extra=(), stream="vector"):
        reads = [in0b] + list(extra)
        writes = [outb] + ([accumb] if accumb is not None else [])
        try:
            sm_flag = out.free_size() <= 128 or accum_out is not None
        except Exception:
            sm_flag = True

        def f(eng):
            kw = dict(out=out, in0=in0, scalar1=s1, scalar2=s2, op0=op0)
            if op1 is not None:
                kw["op1"] = op1
            if accum_out is not None:
                kw["accum_out"] = accum_out
            return eng.tensor_scalar(**kw)
        return O(stream, f, reads=reads, writes=writes, small=sm_flag)

    O("sync", lambda e: e.dma_start(out=idb[:], in_=idb_i[:]),
      writes=[idb_b], dma=True)
    O("sync", lambda e: e.dma_start(out=idf[:], in_=idf_i[:]),
      writes=[idf_b], dma=True)
    O("sync", lambda e: e.dma_start(out=iou[:], in_=iou_i[:]),
      writes=[iou_b], dma=True)
    O("gpsimd", lambda e: e.dma_start(out=bkr_s[:], in_=bkr_i[:]),
      writes=[bkr_b], dma=True)
    O("gpsimd", lambda e: e.dma_start(out=bq_s[:], in_=bq_i[:]),
      writes=[bq_b], dma=True)
    O("gpsimd", lambda e: e.dma_start(out=bvp_s[:], in_=bvp_i[:]),
      writes=[bvp_b], dma=True)
    O("gpsimd", lambda e: e.dma_start(out=bo_s[:], in_=bo_i[:]),
      writes=[bo_b], dma=True)
    O("vector", lambda e: e.memset(ones1b[:], 1.0), writes=[ones1b_b])
    O("vector", lambda e: e.memset(onescb[:], 1.0), writes=[onescb_b], small=True)
    O("vector", lambda e: e.memset(zero_t[:], 0.0), writes=[zero_b])
    O("vector", lambda e: e.memset(eps_s[:], 1e-5), writes=[eps_b], small=True)
    _breg = {}

    def f_breg(eng):
        _breg["r"] = eng.to_reg(SEQ * NH - 1)
        return None
    O("gpsimd", f_breg)

    def quant_core(xu, xu_b, ta, tab, tb, tbb, tc, tcb, dst, dstb,
                   shb, shb_b, bcast_blocks=None, stream="vector"):
        def f_abs(eng):
            return eng.activation(out=ta, in_=xu, func=AF.Abs)
        O("scalar", f_abs, reads=[xu_b], writes=[tab])

        def f_m(eng):
            return eng.tensor_scalar(out=tc.bitcast(I32),
                                     in0=xu.bitcast(I32),
                                     scalar1=0x7F800000, scalar2=None,
                                     op0=OP.bitwise_and)
        O(stream, f_m, reads=[xu_b], writes=[tcb])
        TS(tc, tcb, tc, tcb, 2.0, OP.max, 6291456.0, OP.mult,
           stream=stream)

        def f_ad(eng):
            return eng.tensor_tensor(out=tb, in0=ta, in1=tc, op=OP.add)
        O(stream, f_ad, reads=[tab, tcb], writes=[tbb])

        def f_sb(eng):
            return eng.tensor_tensor(out=tb, in0=tb, in1=tc,
                                     op=OP.subtract)
        O(stream, f_sb, reads=[tbb, tcb], writes=[tbb])

        def f_sg(eng):
            return eng.activation(out=ta, in_=xu, func=AF.Sign)
        O("scalar", f_sg, reads=[xu_b], writes=[tab])

        def f_sm(eng):
            return eng.scalar_tensor_tensor(out=tb, in0=tb, scalar=12.0,
                                            in1=ta, op0=OP.min,
                                            op1=OP.mult)
        O(stream, f_sm, reads=[tbb, tab], writes=[tbb])
        if bcast_blocks is not None:
            nb = bcast_blocks
            shbb = shb.unsqueeze(2).broadcast_to([128, nb, 16])

            def f_dq(eng):
                return eng.tensor_tensor(
                    out=dst.rearrange("p (b k) -> p b k", k=16),
                    in0=tb.rearrange("p (b k) -> p b k", k=16),
                    in1=shbb, op=OP.mult)
        else:
            def f_dq(eng):
                return eng.tensor_tensor(out=dst, in0=tb, in1=shb,
                                         op=OP.mult)
        O(stream, f_dq, reads=[tbb, shb_b], writes=[dstb])

    def quant_tm(src, srcb, dst, dstb, Fdim, ta, tab, tb, tbb, tc, tcb,
                 xu, xu_b, alloc):
        NB = Fdim // 16
        bm, bm_b = alloc([128, NB], F32, "qbm")
        shb, shb_b = alloc([128, NB], BF16, "qshb")
        r2, r2_b = alloc([128, NB], F32, "qr2")
        s3 = src.rearrange("p (b k) -> p b k", k=16)

        def f_red(eng):
            return eng.tensor_reduce(out=bm[:], in_=s3, axis=AX.X,
                                     op=OP.max, apply_absolute_value=True)
        O("vector", f_red, reads=[srcb], writes=[bm_b], small=True)
        TS(bm[:], bm_b, bm[:], bm_b, 1.0 / 12.0, OP.mult, 0.5e-12, OP.max)
        O("vector", lambda e: e.reciprocal(out=r2[:], in_=bm[:]),
          reads=[bm_b], writes=[r2_b], small=True)
        O("vector", lambda e: e.tensor_copy(out=shb[:], in_=bm[:]),
          reads=[bm_b], writes=[shb_b], small=True)
        r2bc = r2.unsqueeze(2).broadcast_to([128, NB, 16])

        def f_xu(eng):
            return eng.tensor_tensor(
                out=xu.rearrange("p (b k) -> p b k", k=16), in0=s3,
                in1=r2bc, op=OP.mult)
        O("vector", f_xu, reads=[srcb, r2_b], writes=[xu_b])
        quant_core(xu, xu_b, ta, tab, tb, tbb, tc, tcb, dst, dstb,
                   shb[:], shb_b, bcast_blocks=NB)

    def ln_scratch(alloc):
        sm, sm_b = alloc([128, 4], F32, "ln_s")
        dmp, dmp_b = alloc([128, D], F32, "ln_d")
        return (sm, sm_b, dmp, dmp_b)

    def emit_ln(xt, xtb, xln, xln_b, sc):
        sm, sm_b, dmp, dmp_b = sc

        def f_sum(eng):
            return eng.activation(out=dmp[:], in_=xt, func=AF.Copy,
                                  accum_out=sm[:, 3:4])
        O("scalar", f_sum, reads=[xtb], writes=[dmp_b, sm_b], small=True)

        def f_sq(eng):
            return eng.activation(out=dmp[:], in_=xt, func=AF.Square,
                                  accum_out=sm[:, 2:3])
        O("scalar", f_sq, reads=[xtb], writes=[dmp_b, sm_b], small=True)
        TS(sm[:, 0:1], sm_b, sm[:, 3:4], sm_b, -1.0 / D, OP.mult)

        def f_m2(eng):
            return eng.activation(out=sm[:, 1:2], in_=sm[:, 0:1],
                                  func=AF.Square)
        O("scalar", f_m2, reads=[sm_b], writes=[sm_b], small=True)

        def f_var(eng):
            return eng.scalar_tensor_tensor(out=sm[:, 1:2], in0=sm[:, 2:3],
                                            scalar=1.0 / D, in1=sm[:, 1:2],
                                            op0=OP.mult, op1=OP.subtract)
        O("vector", f_var, reads=[sm_b], writes=[sm_b], small=True)

        def f_sqrt(eng):
            return eng.activation(out=sm[:, 1:2], in_=sm[:, 1:2],
                                  func=AF.Sqrt, bias=eps_s[:])
        O("scalar", f_sqrt, reads=[sm_b, eps_b], writes=[sm_b], small=True)
        O("vector", lambda e: e.reciprocal(out=sm[:, 1:2], in_=sm[:, 1:2]),
          reads=[sm_b], writes=[sm_b], small=True)

        def f_app(eng):
            return eng.tensor_scalar(out=xln, in0=xt, scalar1=sm[:, 0:1],
                                     scalar2=sm[:, 1:2], op0=OP.add,
                                     op1=OP.mult)
        O("vector", f_app, reads=[xtb, sm_b], writes=[xln_b])

    def emit_t8(src, srcb, dst_fn, dstb):
        pt, pt_b = psb(6)
        for kc in range(8):
            half = kc % 2

            def f_tr(eng, kc=kc, half=half):
                return eng.transpose(pt[:, half * 128:(half + 1) * 128],
                                     src[:, kc * 128:(kc + 1) * 128],
                                     idb[:])
            O("tensor", f_tr, reads=[srcb, idb_b], writes=[pt_b])

            def f_ev(eng, kc=kc, half=half):
                return eng.tensor_copy(out=dst_fn(kc),
                                       in_=pt[:, half * 128:(half + 1) * 128])
            O("vector", f_ev, reads=[pt_b], writes=[dstb])

    pmark = AR.mark()
    for (w_in, wq_dst, wq_dbs) in ((wg_i, wgq_d, wgq_db),
                                   (wu_i, wuq_d, wuq_db)):
        for m in range(24):
            AR.reset(pmark)
            wt, wt_b = AR.alloc([128, D], F32, "pw")
            wq_t, wq_b = AR.alloc([128, D], BF16, "pwq")
            stg, stg_b = AR.alloc([128, 8, 128], BF16, "pst")
            ta, tab = AR.alloc([128, D], F32, "pqa")
            tbx, tbb = AR.alloc([128, D], F32, "pqb")
            tcx, tcb = AR.alloc([128, D], F32, "pqc")
            xu, xu_b = AR.alloc([128, D], F32, "pqx")

            def f_ld(eng, m=m, w_in=w_in, wt=wt):
                return eng.dma_start(out=wt[:],
                                     in_=w_in[m * 128:(m + 1) * 128, :])
            O("sync", f_ld, writes=[wt_b], dma=True)
            quant_tm(wt[:], wt_b, wq_t[:], wq_b, D, ta[:], tab, tbx[:], tbb,
                     tcx[:], tcb, xu[:], xu_b, AR.alloc)
            emit_t8(wq_t, wq_b, lambda kc, stg=stg: stg[:, kc, :], stg_b)

            def f_st(eng, m=m, wq_dst=wq_dst, stg=stg):
                return eng.dma_start(out=wq_dst[m], in_=stg[:])
            O("sync", f_st, reads=[stg_b], writes=[wq_dbs[m]], dma=True)

    for t in range(8):
        AR.reset(pmark)
        wt, wt_b = AR.alloc([128, I_FF], F32, "pwd")
        wq_t, wq_b = AR.alloc([128, I_FF], BF16, "pwdq")
        stg, stg_b = AR.alloc([128, 128], BF16, "pwds")
        ta, tab = AR.alloc([128, I_FF], F32, "pda")
        tbx, tbb = AR.alloc([128, I_FF], F32, "pdb")
        tcx, tcb = AR.alloc([128, I_FF], F32, "pdc")
        xu, xu_b = AR.alloc([128, I_FF], F32, "pdx")

        def f_ld(eng, t=t, wt=wt):
            return eng.dma_start(out=wt[:],
                                 in_=wd_i[t * 128:(t + 1) * 128, :])
        O("sync", f_ld, writes=[wt_b], dma=True)
        quant_tm(wt[:], wt_b, wq_t[:], wq_b, I_FF, ta[:], tab, tbx[:], tbb,
                 tcx[:], tcb, xu[:], xu_b, AR.alloc)
        for c in range(24):
            pt, pt_b = psb(6)
            if c < 16:
                view = wq_t.rearrange("p (b j) -> p j b", j=16)[:, c, 0:128]

                def f_tr(eng, view=view, pt=pt):
                    return eng.transpose(pt[:, 0:128], view, idb[:])
                O("tensor", f_tr, reads=[wq_b, idb_b], writes=[pt_b])

                def f_ev(eng, stg=stg, pt=pt):
                    return eng.tensor_copy(out=stg[:], in_=pt[:, 0:128])
                O("vector", f_ev, reads=[pt_b], writes=[stg_b])
            else:
                jj = c - 16
                O("vector", lambda e, stg=stg: e.memset(stg[:], 0.0),
                  writes=[stg_b])
                vtail = wq_t[:, 2048:2816].rearrange(
                    "p (b j) -> p j b", j=16)
                v0 = vtail[:, 2 * jj, :]
                v1 = vtail[:, 2 * jj + 1, :]

                def f_t0(eng, v0=v0, pt=pt):
                    return eng.transpose(pt[0:48, 0:128], v0, idb[:])
                O("tensor", f_t0, reads=[wq_b, idb_b], writes=[pt_b])

                def f_e0(eng, stg=stg, pt=pt):
                    return eng.tensor_copy(out=stg[0:48, :],
                                           in_=pt[0:48, 0:128])
                O("vector", f_e0, reads=[pt_b], writes=[stg_b])

                def f_t1(eng, v1=v1, pt=pt):
                    return eng.transpose(pt[0:48, 128:256], v1, idb[:])
                O("tensor", f_t1, reads=[wq_b, idb_b], writes=[pt_b])

                def f_e1(eng, stg=stg, pt=pt):
                    return eng.tensor_copy(out=stg[64:112, :],
                                           in_=pt[0:48, 128:256])
                O("vector", f_e1, reads=[pt_b], writes=[stg_b])

            def f_st(eng, c=c, t=t, stg=stg):
                return eng.dma_start(out=wdq_d[c, :, t * 128:(t + 1) * 128],
                                     in_=stg[:])
            O("sync", f_st, reads=[stg_b], writes=[wdq_db[c]], dma=True)

    def emit_ffn(seq, first):
        base = 0
        for g in range(NTG):
            AR.reset(base)
            toks = [g * 4 + i for i in range(4)]
            hf, hf_b = AR.alloc([128, 24, TG], F32, "hf")
            hq, hq_b = AR.alloc([128, 24, TG], BF16, "hq")
            xqf, xqf_b = AR.alloc([128, 8, TG], BF16, "xqf")
            xts = [AR.alloc([128, D], F32, f"xt{i}") for i in range(4)]
            sA, sA_b = AR.alloc([128, TG], F32, "sA")
            sB, sB_b = AR.alloc([128, TG], F32, "sB")
            shAb, shAb_b = AR.alloc([128, TG], BF16, "shAb")
            r2A, r2A_b = AR.alloc([128, TG], F32, "r2A")
            shBb, shBb_b = AR.alloc([64, TG], BF16, "shBb")
            r2B, r2B_b = AR.alloc([64, TG], F32, "r2B")
            ta, tab = AR.alloc([128, D], F32, "qta")
            tbx, tbb = AR.alloc([128, D], F32, "qtb")
            tcx, tcb = AR.alloc([128, D], F32, "qtc")
            xu, xu_b = AR.alloc([128, D], F32, "qxu")
            sg_t, sg_b = AR.alloc([128, TG], F32, "sg")
            sc = ln_scratch(AR.alloc)
            xln, xln_b = AR.alloc([128, D], F32, "xln")
            xq_tm, xq_tm_b = AR.alloc([128, D], BF16, "xqtm")
            qmark = AR.mark()

            for i, ti in enumerate(toks):
                AR.reset(qmark)
                xt, xtb = xts[i]
                src_ap = (x_in[seq * SEQ + ti * 128:
                               seq * SEQ + (ti + 1) * 128, :]
                          if first else
                          x_d[ti * 128:(ti + 1) * 128, :])

                def f_lx(eng, xt=xt, src_ap=src_ap):
                    return eng.dma_start(out=xt[:], in_=src_ap)
                rd = [] if first else [x_db[ti]]
                O("sync", f_lx, reads=rd, writes=[xtb], dma=True)
                emit_ln(xt[:], xtb, xln[:], xln_b, sc)
                quant_tm(xln[:], xln_b, xq_tm[:], xq_tm_b, D, ta[:], tab,
                         tbx[:], tbb, tcx[:], tcb, xu[:], xu_b, AR.alloc)
                emit_t8(xq_tm, xq_tm_b,
                        lambda kc, i=i: xqf[:, kc, i * 128:(i + 1) * 128],
                        xqf_b)

            wmark = AR.mark()
            wg_t = []
            for db2 in range(2):
                a0, b0 = AR.alloc([128, 8, 128], BF16, f"wgt{db2}")
                a1, b1 = AR.alloc([128, 8, 128], BF16, f"wut{db2}")
                wg_t.append(((a0, b0), (a1, b1)))
            for m in range(24):
                (gA, gAb), (uA, uAb) = wg_t[m % 2]

                def f_lg(eng, m=m, gA=gA):
                    return eng.dma_start(out=gA[:], in_=wgq_d[m])
                O("sync", f_lg, reads=[wgq_db[m]], writes=[gAb], dma=True)

                def f_lu(eng, m=m, uA=uA):
                    return eng.dma_start(out=uA[:], in_=wuq_d[m])
                O("scalar", f_lu, reads=[wuq_db[m]], writes=[uAb],
                  dma=True)
                pg, pg_b = psf(0 if m % 2 == 0 else 1)
                pu, pu_b = psf(2 if m % 2 == 0 else 3)
                for kc in range(8):
                    def f_mg(eng, kc=kc, gA=gA, pg=pg):
                        return eng.matmul(pg[:], gA[:, kc, :], xqf[:, kc, :],
                                          start=(kc == 0), stop=(kc == 7))
                    O("tensor", f_mg, reads=[gAb, xqf_b], writes=[pg_b])
                for kc in range(8):
                    def f_mu(eng, kc=kc, uA=uA, pu=pu):
                        return eng.matmul(pu[:], uA[:, kc, :], xqf[:, kc, :],
                                          start=(kc == 0), stop=(kc == 7))
                    O("tensor", f_mu, reads=[uAb, xqf_b], writes=[pu_b])

                def f_sil(eng, pg=pg):
                    return eng.activation(out=sg_t[:], in_=pg[:],
                                          func=AF.Silu)
                O("scalar", f_sil, reads=[pg_b], writes=[sg_b])

                def f_h(eng, m=m, pu=pu):
                    return eng.scalar_tensor_tensor(
                        out=hf[:, m, :], in0=pu[:], scalar=1.0, in1=sg_t[:],
                        op0=OP.mult, op1=OP.mult)
                O("vector", f_h, reads=[pu_b, sg_b], writes=[hf_b])

                def f_abs(eng, m=m):
                    return eng.activation(out=ta[:, 0:TG], in_=hf[:, m, :],
                                          func=AF.Abs)
                O("scalar", f_abs, reads=[hf_b], writes=[tab])
                tgt, tgtb = (sA, sA_b) if m < 16 else (sB, sB_b)
                if m in (0, 16):
                    O("vector",
                      lambda e, tgt=tgt: e.tensor_copy(out=tgt[:],
                                                       in_=ta[:, 0:TG]),
                      reads=[tab], writes=[tgtb])
                else:
                    def f_mx(eng, tgt=tgt):
                        return eng.tensor_tensor(out=tgt[:], in0=tgt[:],
                                                 in1=ta[:, 0:TG], op=OP.max)
                    O("vector", f_mx, reads=[tab, tgtb], writes=[tgtb])

            def f_foldc(eng):
                return eng.tensor_copy(out=ta[0:64, 0:TG],
                                       in_=sB[64:128, :])
            O("vector", f_foldc, reads=[sB_b], writes=[tab])

            def f_fold(eng):
                return eng.tensor_tensor(out=sB[0:64, :], in0=sB[0:64, :],
                                         in1=ta[0:64, 0:TG], op=OP.max)
            O("vector", f_fold, reads=[sB_b, tab], writes=[sB_b])
            TS(sA[:], sA_b, sA[:], sA_b, 1.0 / 12.0, OP.mult, 0.5e-12,
               OP.max)
            TS(sB[0:64, :], sB_b, sB[0:64, :], sB_b, 1.0 / 12.0, OP.mult,
               0.5e-12, OP.max)
            O("vector", lambda e: e.reciprocal(out=r2A[:], in_=sA[:]),
              reads=[sA_b], writes=[r2A_b])
            O("vector", lambda e: e.reciprocal(out=r2B[:], in_=sB[0:64, :]),
              reads=[sB_b], writes=[r2B_b])
            O("vector", lambda e: e.tensor_copy(out=shAb[:], in_=sA[:]),
              reads=[sA_b], writes=[shAb_b])
            O("vector",
              lambda e: e.tensor_copy(out=shBb[:], in_=sB[0:64, :]),
              reads=[sB_b], writes=[shBb_b])

            hcp, hcp_b = AR.alloc([64, TG], F32, "hcp")
            for m in range(24):
                strm = "vector"
                qta, qtab = ta[:, 0:TG], tab
                qtb, qtbb = tbx[:, 0:TG], tbb
                qtc, qtcb = tcx[:, 0:TG], tcb
                qxu, qxub = xu[:, 0:TG], xu_b
                qcp, qcpb = hcp, hcp_b
                if m < 16:
                    def f_xu(eng, m=m, qxu=qxu):
                        return eng.tensor_tensor(out=qxu,
                                                 in0=hf[:, m, :],
                                                 in1=r2A[:], op=OP.mult)
                    O(strm, f_xu, reads=[hf_b, r2A_b], writes=[qxub])
                    quant_core(qxu, qxub, qta, qtab, qtb, qtbb, qtc, qtcb,
                               hq[:, m, :], hq_b, shAb[:], shAb_b,
                               bcast_blocks=None, stream=strm)
                else:
                    for lohi in range(2):
                        if lohi == 0:
                            def f_xu(eng, m=m, qxu=qxu):
                                return eng.tensor_tensor(
                                    out=qxu[0:64, :],
                                    in0=hf[:, m, :][0:64, :],
                                    in1=r2B[:], op=OP.mult)
                            O(strm, f_xu, reads=[hf_b, r2B_b],
                              writes=[qxub])
                        else:
                            def f_cp(eng, m=m, qcp=qcp):
                                return eng.tensor_copy(
                                    out=qcp[:], in_=hf[:, m, :][64:128, :])
                            O(strm, f_cp, reads=[hf_b], writes=[qcpb])

                            def f_xu(eng, qxu=qxu, qcp=qcp):
                                return eng.tensor_tensor(
                                    out=qxu[0:64, :], in0=qcp[:],
                                    in1=r2B[:], op=OP.mult)
                            O(strm, f_xu, reads=[qcpb, r2B_b],
                              writes=[qxub])
                        dsl = slice(0, 64) if lohi == 0 else slice(64, 128)
                        quant_core(qxu[0:64, :], qxub, qta[0:64, :],
                                   qtab, qtb[0:64, :], qtbb,
                                   qtc[0:64, :], qtcb,
                                   hq[:, m, :][dsl, :], hq_b, shBb[:],
                                   shBb_b, bcast_blocks=None, stream=strm)

            wdt = []
            for db2 in range(2):
                a0, b0 = AR.alloc([128, D], BF16, f"wdt{db2}")
                wdt.append((a0, b0))
            for kc in range(24):
                wda, wdab = wdt[kc % 2]

                def f_lw(eng, kc=kc, wda=wda):
                    return eng.dma_start(out=wda[:], in_=wdq_d[kc])
                O("scalar" if kc % 2 else "sync", f_lw,
                  reads=[wdq_db[kc]], writes=[wdab], dma=True)
                for mt in range(4):
                    for nn in range(2):
                        pd, pd_b = psf(mt * 2 + nn)

                        def f_md(eng, kc=kc, mt=mt, nn=nn, wda=wda, pd=pd):
                            return eng.matmul(
                                pd[:], hq[:, kc, mt * 128:(mt + 1) * 128],
                                wda[:, nn * 512:(nn + 1) * 512],
                                start=(kc == 0), stop=(kc == 23))
                        O("tensor", f_md, reads=[hq_b, wdab], writes=[pd_b])
            for mt in range(4):
                ti = toks[mt]
                xt, xtb = xts[mt]
                for nn in range(2):
                    pd, pd_b = psf(mt * 2 + nn)

                    def f_res(eng, nn=nn, pd=pd, xt=xt):
                        return eng.scalar_tensor_tensor(
                            out=xt[:, nn * 512:(nn + 1) * 512], in0=pd[:],
                            scalar=0.5, in1=xt[:, nn * 512:(nn + 1) * 512],
                            op0=OP.mult, op1=OP.add)
                    O("vector", f_res, reads=[pd_b, xtb], writes=[xtb])

                def f_sx(eng, ti=ti, xt=xt):
                    return eng.dma_start(out=x_d[ti * 128:(ti + 1) * 128, :],
                                         in_=xt[:])
                O("sync", f_sx, reads=[xtb], writes=[x_db[ti]], dma=True)

    def emit_attn(seq):
        AR.reset(0)
        norms2, norms2_b = AR.alloc([16, SEQ], F32, "nrm2")
        cums, cums_b = AR.alloc([16, SEQ], F32, "cums")
        bdmp, bdmp_b = AR.alloc([16, SEQ], F32, "bdmp")
        zz16, zz16_b = AR.alloc([16, SEQ], F32, "zz16")
        srcu, srcu_b = AR.alloc([128, 4], U32, "srcu")
        ct_t, ct_b = AR.alloc([128, 16, 16], F32, "ct")
        bs, bs_b = AR.alloc([16, 8], F32, "bis")
        v_sb, v_b = AR.alloc([128, 16, NH * 72], BF16, "v_sb")
        kfm, kfm_b = AR.alloc([128, 8], F32, "kfm")
        kf2, kf2_b = AR.alloc([128, 8], F32, "kf2")
        nbh, nbh_b = AR.alloc([1, 16], F32, "nbh")
        sq16, sq16_b = AR.alloc([1, 16], F32, "sq16")
        km16, km16_b = AR.alloc([1, 16], F32, "km16")
        kdr, kdr_b = AR.alloc([1, 64], F32, "kdr")
        markA = AR.mark()

        xb, xb_b = AR.alloc([128, 8, SEQ], BF16, "xb")
        stg, stg_b = AR.alloc([128, SEQ], BF16, "astg")
        kdu, kdu_b = AR.alloc([128, SEQ], BF16, "kdu")
        qf32, qf32_b = AR.alloc([128, 512], F32, "qf32")
        sqt, sqt_b = AR.alloc([128, D], F32, "asqt")
        nrm_tm, nrm_tm_b = AR.alloc([128, NH], F32, "anrm")
        sc = ln_scratch(AR.alloc)
        xt_a, xt_ab = AR.alloc([128, D], F32, "axt")
        xlnb, xlnb_b = AR.alloc([128, D], BF16, "axln")
        wq_s, wq_sb = AR.alloc([128, 8, D], BF16, "wq_s")

        for ti in range(16):
            def f_lx(eng, ti=ti):
                return eng.dma_start(out=xt_a[:],
                                     in_=x_d[ti * 128:(ti + 1) * 128, :])
            O("sync", f_lx, reads=[x_db[ti]], writes=[xt_ab], dma=True)
            emit_ln(xt_a[:], xt_ab, xlnb[:], xlnb_b, sc)
            emit_t8(xlnb, xlnb_b,
                    lambda kc, ti=ti: xb[:, kc, ti * 128:(ti + 1) * 128],
                    xb_b)

        def f_lwq(eng):
            return eng.dma_start(
                out=wq_s[:], in_=wqt_i.rearrange("(kc p) n -> p kc n",
                                                 p=128))
        O("gpsimd", f_lwq, writes=[wq_sb], dma=True)
        for mt in range(16):
            for nn in range(2):
                pq, pq_b = psf(nn)
                for kc in range(8):
                    def f_m(eng, kc=kc, mt=mt, nn=nn, pq=pq):
                        return eng.matmul(
                            pq[:], xb[:, kc, mt * 128:(mt + 1) * 128],
                            wq_s[:, kc, nn * 512:(nn + 1) * 512],
                            start=(kc == 0), stop=False)
                    O("tensor", f_m, reads=[xb_b, wq_sb], writes=[pq_b])

                def f_b(eng, nn=nn, pq=pq):
                    return eng.matmul(pq[:], ones1b[0:1, 0:128],
                                      bq_s[:, nn * 512:(nn + 1) * 512],
                                      start=False, stop=True)
                O("tensor", f_b, reads=[ones1b_b, bq_b], writes=[pq_b])

                def f_ev1(eng, pq=pq):
                    return eng.activation(out=qf32[:], in_=pq[:],
                                          func=AF.Copy, scale=0.125)
                O("scalar", f_ev1, reads=[pq_b], writes=[qf32_b])

                def f_ev(eng, nn=nn):
                    return eng.activation(
                        out=stg[:, nn * 512:(nn + 1) * 512], in_=qf32[:],
                        func=AF.Copy)
                O("scalar", f_ev, reads=[qf32_b], writes=[stg_b])

                def f_sq2(eng, nn=nn, pq=pq):
                    return eng.activation(
                        out=sqt[:, nn * 512:(nn + 1) * 512], in_=pq[:],
                        func=AF.Square)
                O("scalar", f_sq2, reads=[pq_b], writes=[sqt_b])

            def f_red(eng):
                return eng.tensor_reduce(
                    out=nrm_tm[:],
                    in_=sqt.rearrange("p (h d) -> p h d", d=HD), axis=AX.X,
                    op=OP.add)
            O("vector", f_red, reads=[sqt_b], writes=[nrm_tm_b],
              small=True)
            pt7, pt7_b = psf(7)

            def f_trn(eng, pt7=pt7):
                return eng.transpose(pt7[0:16, 0:128], nrm_tm[:], idf[:])
            O("tensor", f_trn, reads=[nrm_tm_b, idf_b], writes=[pt7_b])

            def f_evn(eng, mt=mt, pt7=pt7):
                return eng.tensor_copy(
                    out=norms2[:, mt * 128:(mt + 1) * 128],
                    in_=pt7[0:16, 0:128])
            O("vector", f_evn, reads=[pt7_b], writes=[norms2_b], small=True)

            def f_stq(eng, mt=mt):
                return eng.dma_start(out=q_d[mt * 128:(mt + 1) * 128, :],
                                     in_=stg[:, 0:D])
            O("sync", f_stq, reads=[stg_b], writes=[q_db[mt]], dma=True)

        def f_lwk(eng):
            return eng.dma_start(
                out=wq_s[:], in_=wkt_i.rearrange("(kc p) n -> p kc n",
                                                 p=128))
        O("gpsimd", f_lwk, writes=[wq_sb], dma=True)
        for mo in range(8):
            for nn in range(4):
                pk, pk_b = psf(2 + (nn % 2))
                for kc in range(8):
                    def f_m(eng, kc=kc, mo=mo, nn=nn, pk=pk):
                        return eng.matmul(
                            pk[:], wq_s[:, kc, mo * 128:(mo + 1) * 128],
                            xb[:, kc, nn * 512:(nn + 1) * 512],
                            start=(kc == 0), stop=False)
                    O("tensor", f_m, reads=[xb_b, wq_sb], writes=[pk_b])

                def f_kb(eng, mo=mo, pk=pk):
                    return eng.matmul(pk[:],
                                      bkr_s[0:1, mo * 128:(mo + 1) * 128],
                                      ones1b[:], start=False, stop=True)
                O("tensor", f_kb, reads=[bkr_b, ones1b_b], writes=[pk_b])

                def f_ev(eng, nn=nn, pk=pk):
                    return eng.activation(
                        out=stg[:, nn * 512:(nn + 1) * 512], in_=pk[:],
                        func=AF.Copy)
                O("scalar", f_ev, reads=[pk_b], writes=[stg_b])
            for hh in range(2):
                def f_st(eng, mo=mo, hh=hh):
                    return eng.dma_start(out=k_d[mo * 2 + hh],
                                         in_=stg[hh * 64:(hh + 1) * 64, :])
                O("sync", f_st, reads=[stg_b], writes=[k_db[mo * 2 + hh]],
                  dma=True)
            TS(kdu[:], kdu_b, stg[:, 0:SEQ], stg_b, 1.0, OP.mult, -1e30,
               OP.max, accum_out=kfm[:, mo:mo + 1], accumb=kfm_b)
            TS(kdu[:], kdu_b, stg[:, 0:SEQ], stg_b, -1.0, OP.mult, -1e30,
               OP.max, accum_out=kf2[:, mo:mo + 1], accumb=kf2_b)

            def f_mg2(eng, mo=mo):
                return eng.tensor_tensor(out=kfm[:, mo:mo + 1],
                                         in0=kfm[:, mo:mo + 1],
                                         in1=kf2[:, mo:mo + 1], op=OP.max)
            O("vector", f_mg2, reads=[kfm_b, kf2_b], writes=[kfm_b],
              small=True)
            pt6k, pt6k_b = psf(6)

            def f_tk(eng, mo=mo, pt6k=pt6k):
                return eng.transpose(pt6k[0:1, 0:128], kfm[:, mo:mo + 1],
                                     idf[:])
            O("tensor", f_tk, reads=[kfm_b, idf_b], writes=[pt6k_b])
            TS(kdr[:], kdr_b, pt6k[0:1, 0:64], pt6k_b, 1.0, OP.mult,
               -1e30, OP.max, accum_out=km16[0:1, 2 * mo:2 * mo + 1],
               accumb=km16_b)
            TS(kdr[:], kdr_b, pt6k[0:1, 64:128], pt6k_b, 1.0, OP.mult,
               -1e30, OP.max, accum_out=km16[0:1, 2 * mo + 1:2 * mo + 2],
               accumb=km16_b)

        wv_s, wv_sb = AR.alloc([128, 8, NH * 72], BF16, "wv_s")

        def f_lwv(eng):
            return eng.dma_start(
                out=wv_s[:], in_=wvtp_i.rearrange("(kc p) n -> p kc n",
                                                  p=128))
        O("gpsimd", f_lwv, writes=[wv_sb], dma=True)
        for mt in range(16):
            for nn in range(3):
                n0 = nn * 512
                n1 = min((nn + 1) * 512, NH * 72)
                pv, pv_b = psf(4 + (nn % 2))
                for kc in range(8):
                    def f_m(eng, kc=kc, mt=mt, n0=n0, n1=n1, pv=pv):
                        return eng.matmul(
                            pv[:, 0:n1 - n0],
                            xb[:, kc, mt * 128:(mt + 1) * 128],
                            wv_s[:, kc, n0:n1], start=(kc == 0), stop=False)
                    O("tensor", f_m, reads=[xb_b, wv_sb], writes=[pv_b])

                def f_b(eng, n0=n0, n1=n1, pv=pv):
                    return eng.matmul(pv[:, 0:n1 - n0], ones1b[0:1, 0:128],
                                      bvp_s[:, n0:n1], start=False,
                                      stop=True)
                O("tensor", f_b, reads=[ones1b_b, bvp_b], writes=[pv_b])

                def f_ev(eng, mt=mt, n0=n0, n1=n1, pv=pv):
                    return eng.activation(out=v_sb[:, mt, n0:n1],
                                          in_=pv[:, 0:n1 - n0],
                                          func=AF.Copy)
                O("scalar", f_ev, reads=[pv_b], writes=[v_b])

        lo_t, hi_t, mid_t = bs[:, 0:1], bs[:, 1:2], bs[:, 2:3]
        cnt_t, msk_t, th_t = bs[:, 3:4], bs[:, 4:5], bs[:, 5:6]
        O("vector", lambda e: e.memset(zz16[:], 0.0), writes=[zz16_b])
        O("vector", lambda e: e.memset(bs[:, 0:1], 0.0), writes=[bs_b], small=True)

        def f_mx0(eng):
            return eng.tensor_reduce(out=bs[:, 1:2], in_=norms2[:],
                                     axis=AX.X, op=OP.max)
        O("vector", f_mx0, reads=[norms2_b], writes=[bs_b], small=True)
        TS(hi_t, bs_b, hi_t, bs_b, 1.0001, OP.mult, 1e-6, OP.add)
        O("vector", lambda e: e.tensor_copy(out=bs[:, 7:8], in_=hi_t),
          reads=[bs_b], writes=[bs_b], small=True)
        for it in range(BISECT_ITERS):
            def f_mid(eng):
                return eng.tensor_tensor(out=mid_t, in0=lo_t, in1=hi_t,
                                         op=OP.add)
            O("vector", f_mid, reads=[bs_b], writes=[bs_b], small=True)
            TS(mid_t, bs_b, mid_t, bs_b, 0.5, OP.mult)
            TS(bdmp[:], bdmp_b, norms2[:], norms2_b, mid_t, OP.is_gt, 0.0,
               OP.add, accum_out=cnt_t, accumb=bs_b, extra=[bs_b])
            TS(msk_t, bs_b, cnt_t, bs_b, float(TOPU), OP.is_ge)
            tmp_a, tmp_b2 = bs[:, 6:7], bs_b

            def f_d1(eng):
                return eng.tensor_tensor(out=tmp_a, in0=mid_t, in1=lo_t,
                                         op=OP.subtract)
            O("vector", f_d1, reads=[bs_b], writes=[bs_b], small=True)

            def f_d2(eng):
                return eng.tensor_tensor(out=tmp_a, in0=tmp_a, in1=msk_t,
                                         op=OP.mult)
            O("vector", f_d2, reads=[bs_b], writes=[bs_b], small=True)

            def f_d3(eng):
                return eng.tensor_tensor(out=lo_t, in0=lo_t, in1=tmp_a,
                                         op=OP.add)
            O("vector", f_d3, reads=[bs_b], writes=[bs_b], small=True)
            TS(msk_t, bs_b, msk_t, bs_b, -1.0, OP.mult, 1.0, OP.add)

            def f_e1(eng):
                return eng.tensor_tensor(out=tmp_a, in0=mid_t, in1=hi_t,
                                         op=OP.subtract)
            O("vector", f_e1, reads=[bs_b], writes=[bs_b], small=True)

            def f_e2(eng):
                return eng.tensor_tensor(out=tmp_a, in0=tmp_a, in1=msk_t,
                                         op=OP.mult)
            O("vector", f_e2, reads=[bs_b], writes=[bs_b], small=True)

            def f_e3(eng):
                return eng.tensor_tensor(out=hi_t, in0=hi_t, in1=tmp_a,
                                         op=OP.add)
            O("vector", f_e3, reads=[bs_b], writes=[bs_b], small=True)

        def f_mask(eng):
            return eng.scalar_tensor_tensor(out=bdmp[:], in0=norms2[:],
                                            scalar=hi_t, in1=norms2[:],
                                            op0=OP.is_le, op1=OP.mult)
        O("vector", f_mask, reads=[norms2_b, bs_b], writes=[bdmp_b])

        def f_th(eng):
            return eng.tensor_reduce(out=th_t, in_=bdmp[:], axis=AX.X,
                                     op=OP.max)
        O("vector", f_th, reads=[bdmp_b], writes=[bs_b], small=True)
        TS(bdmp[:], bdmp_b, norms2[:], norms2_b, th_t, OP.is_ge,
           extra=[bs_b])

        def f_scan(eng):
            return eng.tensor_tensor_scan(out=cums[:], data0=bdmp[:],
                                          data1=zz16[:], initial=0.0,
                                          op0=OP.add, op1=OP.add)
        O("vector", f_scan, reads=[bdmp_b, zz16_b], writes=[cums_b])

        for lc in range(16):
            pt7, pt7_b = psf(7)

            def f_tr(eng, lc=lc, pt7=pt7):
                return eng.transpose(pt7[0:128, 0:16],
                                     cums[:, lc * 128:(lc + 1) * 128],
                                     idf[0:16, 0:16])
            O("tensor", f_tr, reads=[cums_b, idf_b], writes=[pt7_b])

            def f_ev(eng, lc=lc, pt7=pt7):
                return eng.tensor_copy(out=ct_t[:, lc, :],
                                       in_=pt7[0:128, 0:16])
            O("vector", f_ev, reads=[pt7_b], writes=[ct_b], small=True)

        for ti in range(16):
            def f_z(eng, ti=ti):
                return eng.dma_start(out=of_d[ti * 128:(ti + 1) * 128, :],
                                     in_=zero_t[:])
            O("sync", f_z, reads=[zero_b], writes=[of_zb[ti]], dma=True)

        AR.reset(markA)
        smat, smat_b = AR.alloc([128, U], BF16, "smat")
        srcf, srcf_b = AR.alloc([1, U], F32, "srcf")
        kp = []
        for db2 in range(2):
            a0, b0 = AR.alloc([65, SEQ], BF16, f"kp{db2}")
            O("vector", lambda e, a0=a0: e.memset(a0[64:65, :], 1.0),
              writes=[b0])
            kp.append((a0, b0))
        qsel, qsel_b = AR.alloc([128, 4, HD], BF16, "qsel")
        qst, qst_b = AR.alloc([65, U], BF16, "qst")
        att, att_b = AR.alloc([128, 16, U], BF16, "att")
        ostg, ostg_b = AR.alloc([128, HD], BF16, "ostg")
        mxs, mxs_b = AR.alloc([128, 4], F32, "mxs")

        pt7, pt7_b = psf(7)

        def f_tn2(eng, pt7=pt7):
            return eng.transpose(pt7[0:1, 0:16], bs[0:16, 7:8],
                                 idf[0:16, 0:16])
        O("tensor", f_tn2, reads=[bs_b, idf_b], writes=[pt7_b])

        def f_en2(eng, pt7=pt7):
            return eng.tensor_copy(out=sq16[:], in_=pt7[0:1, 0:16])
        O("vector", f_en2, reads=[pt7_b], writes=[sq16_b], small=True)

        def f_sqr(eng):
            return eng.activation(out=sq16[:], in_=sq16[:], func=AF.Sqrt)
        O("scalar", f_sqr, reads=[sq16_b], writes=[sq16_b], small=True)

        def f_nb(eng):
            return eng.tensor_tensor(out=nbh[:], in0=sq16[:], in1=km16[:],
                                     op=OP.mult)
        O("vector", f_nb, reads=[sq16_b, km16_b], writes=[nbh_b],
          small=True)
        TS(nbh[:], nbh_b, nbh[:], nbh_b, -1.0, OP.mult)

        for h in range(NH):
            kpa, kpa_b = kp[h % 2]
            pt7, pt7_b = psf(7)
            for lc in range(16):
                def f_s(eng, lc=lc, h=h):
                    return eng.tensor_scalar(
                        out=smat[:], in0=iou[:],
                        scalar1=ct_t[:, lc, h:h + 1], scalar2=None,
                        op0=OP.is_ge)
                O("vector", f_s, reads=[iou_b, ct_b], writes=[smat_b])

                def f_mm(eng, lc=lc, pt7=pt7):
                    return eng.matmul(pt7[0:1, 0:U], onescb[:], smat[:],
                                      start=(lc == 0), stop=(lc == 15))
                O("tensor", f_mm, reads=[onescb_b, smat_b], writes=[pt7_b])

            def f_evs(eng, pt7=pt7):
                return eng.tensor_copy(out=srcf[:], in_=pt7[0:1, 0:U])
            O("vector", f_evs, reads=[pt7_b], writes=[srcf_b],
              small=True)
            TS(srcf[:], srcf_b, srcf[:], srcf_b, 16.0, OP.mult, float(h),
               OP.add)
            for uc in range(4):
                pt6, pt6_b = psf(6)

                def f_tr(eng, uc=uc, pt6=pt6):
                    return eng.transpose(pt6[0:128, 0:1],
                                         srcf[:, uc * 128:(uc + 1) * 128],
                                         idf[0:1, 0:1])
                O("tensor", f_tr, reads=[srcf_b, idf_b], writes=[pt6_b])

                def f_cv(eng, uc=uc, pt6=pt6):
                    return eng.tensor_copy(out=srcu[:, uc:uc + 1],
                                           in_=pt6[0:128, 0:1])
                O("vector", f_cv, reads=[pt6_b], writes=[srcu_b], small=True)

            def f_lk(eng, h=h, kpa=kpa):
                return eng.dma_start(out=kpa[0:64, :], in_=k_d[h])
            O("sync", f_lk, reads=[k_db[h]], writes=[kpa_b], dma=True)
            O("vector", lambda e: e.memset(qsel[:], 0.0), writes=[qsel_b])
            for uc in range(4):
                def f_g(eng, h=h, uc=uc):
                    return eng.indirect_dma_start(
                        out=qsel[:, uc, :], out_offset=None,
                        in_=q_d.rearrange("s (h d) -> (s h) d", d=HD),
                        in_offset=IndirectOffsetOnAxis(
                            ap=srcu[:, uc:uc + 1], axis=0),
                        bounds_check=_breg["r"], oob_is_err=False)
                O("gpsimd", f_g, reads=list(q_db) + [srcu_b],
                  writes=[qsel_b], dma=True)
            for uc in range(4):
                pt6, pt6_b = psb(5)

                def f_tr(eng, uc=uc, pt6=pt6):
                    return eng.transpose(pt6[0:64, 0:128], qsel[:, uc, :],
                                         idb[:])
                O("tensor", f_tr, reads=[qsel_b, idb_b], writes=[pt6_b])

                def f_ev(eng, uc=uc, pt6=pt6):
                    return eng.tensor_copy(
                        out=qst[0:64, uc * 128:(uc + 1) * 128],
                        in_=pt6[0:64, 0:128])
                O("vector", f_ev, reads=[pt6_b], writes=[qst_b])

            def f_st64(eng, h=h):
                return eng.tensor_scalar(out=qst[64:65, :],
                                         in0=ones1b[0:1, 0:U],
                                         scalar1=nbh[0:1, h:h + 1],
                                         scalar2=None, op0=OP.mult)
            O("vector", f_st64, reads=[ones1b_b, nbh_b], writes=[qst_b],
              small=True)

            for lc in range(16):
                p2, p2_b = psf(2 + lc % 2)

                def f_m(eng, lc=lc, p2=p2, kpa=kpa):
                    return eng.matmul(p2[:],
                                      kpa[:, lc * 128:(lc + 1) * 128],
                                      qst[:], start=True, stop=True)
                O("tensor", f_m, reads=[kpa_b, qst_b], writes=[p2_b])

                def f_e(eng, lc=lc, p2=p2):
                    return eng.activation(out=att[:, lc, :], in_=p2[:],
                                          func=AF.Exp)
                O("scalar", f_e, reads=[p2_b], writes=[att_b])
            for uc in range(4):
                p4, p4_b = psf(4)
                for lc in range(16):
                    def f_m(eng, uc=uc, lc=lc, p4=p4, h=h):
                        return eng.matmul(
                            p4[:, 0:72],
                            att[:, lc, uc * 128:(uc + 1) * 128],
                            v_sb[:, lc, h * 72:(h + 1) * 72],
                            start=(lc == 0), stop=(lc == 15))
                    O("tensor", f_m, reads=[att_b, v_b], writes=[p4_b])

                def f_rc(eng, p4=p4):
                    return eng.reciprocal(out=mxs[:, 0:1], in_=p4[:, 64:65])
                O("vector", f_rc, reads=[p4_b], writes=[mxs_b], small=True)
                TS(ostg[:], ostg_b, p4[:, 0:64], p4_b, mxs[:, 0:1],
                   OP.mult, extra=[mxs_b])

                def f_sc(eng, h=h, uc=uc):
                    return eng.indirect_dma_start(
                        out=of_d.rearrange("s (h d) -> (s h) d", d=HD),
                        out_offset=IndirectOffsetOnAxis(
                            ap=srcu[:, uc:uc + 1], axis=0),
                        in_=ostg[:], in_offset=None,
                        bounds_check=_breg["r"], oob_is_err=False)
                O("gpsimd", f_sc, reads=[ostg_b, srcu_b] + list(of_zb),
                  writes=[of_db[h]], dma=True)

        AR.reset(markA)
        off, off_b = AR.alloc([128, 8, SEQ], BF16, "offm")
        wo_s, wo_sb = AR.alloc([128, 8, D], BF16, "wo_s")
        xt_o, xt_ob = AR.alloc([128, D], F32, "oxt")

        def f_lwo(eng):
            return eng.dma_start(
                out=wo_s[:], in_=wot_i.rearrange("(kc p) n -> p kc n",
                                                 p=128))
        O("gpsimd", f_lwo, writes=[wo_sb], dma=True)
        for kc in range(8):
            def f_t(eng, kc=kc):
                return eng.dma_start(
                    out=off[:, kc, :],
                    in_=of_d[:, kc * 128:(kc + 1) * 128], transpose=True)
            O("sync", f_t, reads=list(of_db) + list(of_zb), writes=[off_b],
              dma=True)
        for mt in range(16):
            def f_lx(eng, mt=mt):
                return eng.dma_start(out=xt_o[:],
                                     in_=x_d[mt * 128:(mt + 1) * 128, :])
            O("sync", f_lx, reads=[x_db[mt]], writes=[xt_ob], dma=True)
            for nn in range(2):
                po, po_b = psf(nn)
                for kc in range(8):
                    def f_m(eng, kc=kc, mt=mt, nn=nn, po=po):
                        return eng.matmul(
                            po[:], off[:, kc, mt * 128:(mt + 1) * 128],
                            wo_s[:, kc, nn * 512:(nn + 1) * 512],
                            start=(kc == 0), stop=False)
                    O("tensor", f_m, reads=[off_b, wo_sb], writes=[po_b])

                def f_b(eng, nn=nn, po=po):
                    return eng.matmul(po[:], ones1b[0:1, 0:128],
                                      bo_s[:, nn * 512:(nn + 1) * 512],
                                      start=False, stop=True)
                O("tensor", f_b, reads=[ones1b_b, bo_b], writes=[po_b])

                def f_res(eng, nn=nn, po=po):
                    return eng.scalar_tensor_tensor(
                        out=xt_o[:, nn * 512:(nn + 1) * 512], in0=po[:],
                        scalar=1.0, in1=xt_o[:, nn * 512:(nn + 1) * 512],
                        op0=OP.mult, op1=OP.add)
                O("vector", f_res, reads=[po_b, xt_ob], writes=[xt_ob])

            def f_sx(eng, mt=mt):
                return eng.dma_start(out=x_d[mt * 128:(mt + 1) * 128, :],
                                     in_=xt_o[:])
            O("sync", f_sx, reads=[xt_ob], writes=[x_db[mt]], dma=True)

    AR.reset(0)
    for seq in range(SPC):
        emit_ffn(seq, first=True)
        emit_attn(seq)
        emit_ffn(seq, first=False)
        for ti in range(16):
            def f_out(eng, seq=seq, ti=ti):
                r0 = seq * SEQ + ti * 128
                return eng.dma_start(out=out_d[r0:r0 + 128, :],
                                     in_=x_d[ti * 128:(ti + 1) * 128, :])
            O("sync", f_out, reads=[x_db[ti]],
              writes=[out_db[seq * 16 + ti]], dma=True)

    O("vector", lambda e: e.nop(), reads=list(out_db))

    with nc.Block() as block:
        bld.finalize(nc, ctx, block)
    ctx.close()
    return nc


_NC_CACHE = None


def _get_nc():
    global _NC_CACHE
    if _NC_CACHE is None:
        _NC_CACHE = build_program()
    return _NC_CACHE


def kernel(L0_attn_x, L0_attn_Wq, L0_attn_bq, L0_attn_Wk, L0_attn_bk,
           L0_attn_Wv, L0_attn_bv, L0_attn_Wo, L0_attn_bo, L0_ffn_w_gate,
           L0_ffn_w_up, L0_ffn_w_down):
    x = np.ascontiguousarray(np.asarray(L0_attn_x, dtype=np.float32))
    Wq = np.asarray(L0_attn_Wq, np.float32)
    Wk = np.asarray(L0_attn_Wk, np.float32)
    Wv = np.asarray(L0_attn_Wv, np.float32)
    Wo = np.asarray(L0_attn_Wo, np.float32)
    bq = np.asarray(L0_attn_bq, np.float32)
    bk = np.asarray(L0_attn_bk, np.float32)
    bv = np.asarray(L0_attn_bv, np.float32)
    bo = np.asarray(L0_attn_bo, np.float32)
    wg = np.asarray(L0_ffn_w_gate, np.float32)
    wu = np.asarray(L0_ffn_w_up, np.float32)
    wd = np.asarray(L0_ffn_w_down, np.float32)

    pos2g = _gf_pos2g()
    valid = pos2g >= 0
    wg_p = np.zeros((3072, D), np.float32)
    wu_p = np.zeros((3072, D), np.float32)
    wg_p[valid] = wg[pos2g[valid]]
    wu_p[valid] = wu[pos2g[valid]]

    wvT = np.ascontiguousarray(Wv.T)
    wvtp = np.zeros((D, NH * 72), np.float32)
    bvp = np.zeros((1, NH * 72), np.float32)
    for h in range(NH):
        wvtp[:, h * 72:h * 72 + 64] = wvT[:, h * 64:(h + 1) * 64]
        bvp[0, h * 72:h * 72 + 64] = bv[h * 64:(h + 1) * 64]
        bvp[0, h * 72 + 64] = 1.0

    common = {
        "wqt": np.ascontiguousarray(Wq.T),
        "wkt": np.ascontiguousarray(Wk.T),
        "wvtp": wvtp,
        "wot": np.ascontiguousarray(Wo.T),
        "bq": bq.reshape(1, D),
        "bkr": bk.reshape(1, D),
        "bvp": bvp,
        "bo": bo.reshape(1, D),
        "wg": wg_p,
        "wu": wu_p,
        "wd": wd,
        "idb": np.eye(128, dtype=ml_dtypes.bfloat16),
        "idf": np.eye(128, dtype=np.float32),
        "iou": np.tile(np.arange(U, dtype=np.float32), (128, 1)),
    }
    nc = _get_nc()
    in_maps = []
    for c in range(NCORES):
        m = dict(common)
        m["x"] = np.ascontiguousarray(
            x[c * SPC:(c + 1) * SPC].reshape(SPC * SEQ, D))
        in_maps.append(m)
    res = run_bass_kernel_spmd(nc, in_maps, core_ids=list(range(NCORES)))
    outs = [res.results[c]["out"].reshape(SPC, SEQ, D)
            for c in range(NCORES)]
    return np.concatenate(outs, axis=0)


# version 19 (final)
# speedup vs baseline: 1.1166x; 1.1166x over previous
import contextlib
import numpy as np
import ml_dtypes
import concourse.bass as bass
import concourse.mybir as mybir
from concourse.bass import IndirectOffsetOnAxis
from concourse.bass_utils import run_bass_kernel_spmd

F32 = mybir.dt.float32
BF16 = mybir.dt.bfloat16
U32 = mybir.dt.uint32
I32 = mybir.dt.int32
OP = mybir.AluOpType
AF = mybir.ActivationFunctionType
AX = mybir.AxisListType

NCORES = 8
B = 32
SEQ = 2048
D = 1024
I_FF = 2816
NH = 16
HD = 64
TOPU = 409
U = 512
SPC = B // NCORES
TG = 512
NTG = SEQ // TG
RNE1 = 12582912.0
RNE2 = 25165824.0
RNE4 = 50331648.0
BISECT_ITERS = 24


def _gf_pos2g():
    pos2g = -np.ones(3072, dtype=np.int64)
    for c in range(16):
        for p in range(128):
            pos2g[c * 128 + p] = 16 * p + c
    for jj in range(8):
        c = 16 + jj
        for p in range(48):
            pos2g[c * 128 + p] = 16 * (128 + p) + 2 * jj
        for p in range(64, 112):
            pos2g[c * 128 + p] = 16 * (128 + p - 64) + 2 * jj + 1
    return pos2g


class Buf:
    __slots__ = ("name", "last_w", "readers", "preds")

    def __init__(self, name=""):
        self.name = name
        self.last_w = None
        self.readers = []
        self.preds = []


class Node:
    __slots__ = ("stream", "fn", "marked", "seq", "is_dma", "deps",
                 "small", "drain_before", "dma_slot", "dma_round")

    def __init__(self, stream, fn, is_dma, small=False):
        self.stream = stream
        self.fn = fn
        self.is_dma = is_dma
        self.marked = False
        self.seq = -1
        self.deps = []
        self.small = small
        self.drain_before = False
        self.dma_slot = -1
        self.dma_round = 0


STREAMS = ("sync", "vector", "scalar", "tensor", "gpsimd")


class Builder:
    def __init__(self):
        self.by_stream = {s: [] for s in STREAMS}
        self.last_drain = {s: -1 for s in STREAMS}

    def op(self, stream, fn, reads=(), writes=(), is_dma=False, small=False):
        n = Node(stream, fn, is_dma, small=small)
        n.seq = len(self.by_stream[stream])
        self.by_stream[stream].append(n)

        def dep_on(p):
            if p is None or p is n:
                return
            if p.stream == stream and not p.is_dma and not n.is_dma:
                if p.small and p.seq >= self.last_drain[stream]:
                    n.drain_before = True
                    self.last_drain[stream] = n.seq
                return
            n.deps.append(p)

        for b in reads:
            for pb in b.preds:
                dep_on(pb.last_w)
            dep_on(b.last_w)
        for b in writes:
            for pb in b.preds:
                dep_on(pb.last_w)
                for r in pb.readers:
                    dep_on(r)
            dep_on(b.last_w)
            for r in b.readers:
                dep_on(r)
        for b in reads:
            b.readers.append(n)
        for b in writes:
            b.last_w = n
            b.readers = []
            b.preds = []
        return n

    def finalize(self, nc, ctx, block):
        for s in STREAMS:
            for n in self.by_stream[s]:
                if n.is_dma:
                    n.marked = True
                for d in n.deps:
                    d.marked = True
        NPOOL = {"sync": 30, "gpsimd": 16, "scalar": 8, "vector": 8,
                 "tensor": 8}
        cum = {}
        sem_c, sem_p = {}, {}
        for s in STREAMS:
            c = 0
            cc = []
            nd = 0
            for n in self.by_stream[s]:
                if n.is_dma:
                    n.dma_slot = nd % NPOOL[s]
                    n.dma_round = nd // NPOOL[s] + 1
                    nd += 1
                elif n.marked:
                    c += 1
                cc.append(c)
            cum[s] = cc
            if c:
                sem_c[s] = ctx.enter_context(nc.semaphore(f"sc_{s}"))
            if nd:
                sem_p[s] = [ctx.enter_context(nc.semaphore(f"sp_{s}{i}"))
                            for i in range(min(nd, NPOOL[s]))]

        def mk(s):
            def run(eng):
                hw = {}
                for n in self.by_stream[s]:
                    if n.drain_before:
                        eng.drain()
                    if n.is_dma and n.dma_round > 1:
                        key = (s, "d", n.dma_slot)
                        val = 16 * (n.dma_round - 1)
                        if hw.get(key, 0) < val:
                            hw[key] = val
                            eng.wait_ge(sem_p[s][n.dma_slot], val)
                    for dep in n.deps:
                        ds = dep.stream
                        if dep.is_dma:
                            key = (ds, "d", dep.dma_slot)
                            val = 16 * dep.dma_round
                            sem = sem_p[ds][dep.dma_slot]
                        else:
                            key, val = (ds, 0), cum[ds][dep.seq]
                            sem = sem_c[ds]
                        if hw.get(key, 0) >= val:
                            continue
                        hw[key] = val
                        eng.wait_ge(sem, val)
                    inst = n.fn(eng)
                    if n.is_dma:
                        inst.then_inc(sem_p[s][n.dma_slot], 16)
                    elif n.marked:
                        assert inst is not None, \
                            f"marked node on {s} seq {n.seq} returned None"
                        inst.then_inc(sem_c[s], 1)
            return run

        for s in STREAMS:
            if self.by_stream[s]:
                getattr(block, s)(mk(s))


class Arena:
    def __init__(self, nc, name, nbytes):
        self.nc, self.name, self.nbytes = nc, name, nbytes
        self.off = 0
        self.live = []
        self.dead = []

    def materialize(self, ctx):
        self.t = ctx.enter_context(
            self.nc.sbuf_tensor(self.name, [128, self.nbytes],
                                mybir.dt.uint8))

    def alloc(self, shape, dtype, tag=""):
        esz = mybir.dt.size(dtype)
        fre = int(np.prod(shape[1:]))
        start = (self.off + 31) // 32 * 32
        end = start + fre * esz
        assert end <= self.nbytes, \
            f"arena {self.name} overflow {end} > {self.nbytes} at {tag}"
        self.off = end
        buf = Buf(f"{self.name}:{tag}")
        buf.preds = [b for (s0, e0, b) in self.live + self.dead
                     if not (e0 <= start or s0 >= end)]
        self.live.append((start, end, buf))
        self.dead = [(s0, e0, b) for (s0, e0, b) in self.dead
                     if not (start <= s0 and e0 <= end)]
        ap = self.t[:, start:end].bitcast(dtype)
        if len(shape) > 2:
            dims = " ".join(f"d{i}" for i in range(1, len(shape)))
            kw = {f"d{i}": shape[i] for i in range(2, len(shape))}
            ap = ap.rearrange(f"p ({dims}) -> p {dims}", **kw)
        if shape[0] != 128:
            ap = ap[0:shape[0]]
        return ap, buf

    def mark(self):
        return self.off

    def reset(self, mark):
        self.off = mark
        self.dead += [(s, e, b) for (s, e, b) in self.live if e > mark]
        self.live = [(s, e, b) for (s, e, b) in self.live if e <= mark]


def build_program():
    nc = bass.Bass("TRN2", target_bir_lowering=False)

    def din(name, shape, dtype):
        return nc.dram_tensor(name, shape, dtype, kind="ExternalInput").ap()

    def dint(name, shape, dtype):
        return nc.dram_tensor(name, shape, dtype, kind="Internal").ap()

    x_in = din("x", [SPC * SEQ, D], F32)
    wqt_i = din("wqt", [D, D], F32)
    wkt_i = din("wkt", [D, D], F32)
    wvtp_i = din("wvtp", [D, NH * 72], F32)
    wot_i = din("wot", [D, D], F32)
    bq_i = din("bq", [1, D], F32)
    bkr_i = din("bkr", [1, D], F32)
    bvp_i = din("bvp", [1, NH * 72], F32)
    bo_i = din("bo", [1, D], F32)
    wg_i = din("wg", [3072, D], F32)
    wu_i = din("wu", [3072, D], F32)
    wd_i = din("wd", [D, I_FF], F32)
    idb_i = din("idb", [128, 128], BF16)
    idf_i = din("idf", [128, 128], F32)
    iou_i = din("iou", [128, U], F32)
    out_d = nc.dram_tensor("out", [SPC * SEQ, D], F32,
                           kind="ExternalOutput").ap()

    wgq_d = dint("wgq_d", [24, 128, 8, 128], BF16)
    wuq_d = dint("wuq_d", [24, 128, 8, 128], BF16)
    wdq_d = dint("wdq_d", [24, 128, D], BF16)
    x_d = dint("x_d", [SEQ, D], F32)
    q_d = dint("q_d", [SEQ, D], BF16)
    k_d = dint("k_d", [NH, HD, SEQ], BF16)
    of_d = dint("of_d", [SEQ, D], BF16)

    wgq_db = [Buf(f"wgq{m}") for m in range(24)]
    wuq_db = [Buf(f"wuq{m}") for m in range(24)]
    wdq_db = [Buf(f"wdq{m}") for m in range(24)]
    x_db = [Buf(f"x_d{i}") for i in range(16)]
    q_db = [Buf(f"q_d{i}") for i in range(16)]
    k_db = [Buf(f"k_d{h}") for h in range(NH)]
    of_db = [Buf(f"of{h}") for h in range(NH)]
    of_zb = [Buf(f"ofz{i}") for i in range(16)]
    out_db = [Buf(f"out{i}") for i in range(SPC * 16)]

    bld = Builder()
    ctx = contextlib.ExitStack()

    def sbuf(name, shape, dtype):
        ap = ctx.enter_context(nc.sbuf_tensor(name, shape, dtype))
        return ap, Buf(name)

    idb, idb_b = sbuf("idb_s", [128, 128], BF16)
    idf, idf_b = sbuf("idf_s", [128, 128], F32)
    iou, iou_b = sbuf("iou_s", [128, U], F32)
    bq_s, bq_b = sbuf("bq_s", [1, D], BF16)
    bkr_s, bkr_b = sbuf("bkr_s", [1, D], BF16)
    bvp_s, bvp_b = sbuf("bvp_s", [1, NH * 72], BF16)
    bo_s, bo_b = sbuf("bo_s", [1, D], BF16)
    ones1b, ones1b_b = sbuf("ones1b", [1, 512], BF16)
    onescb, onescb_b = sbuf("onescb", [128, 1], BF16)
    zero_t, zero_b = sbuf("zero_t", [128, D], BF16)
    eps_s, eps_b = sbuf("eps_s", [128, 1], F32)

    AR = Arena(nc, "arena", 170 * 1024)
    AR.materialize(ctx)

    PS = []
    for i in range(8):
        ap = ctx.enter_context(nc.psum_tensor(f"ps{i}", [128, 512], F32))
        PS.append((ap, Buf(f"ps{i}")))

    def psf(i):
        return PS[i]

    def psb(i):
        ap, b = PS[i]
        return ap[:].bitcast(BF16), b

    def O(stream, fn, reads=(), writes=(), dma=False, small=False):
        return bld.op(stream, fn, reads=reads, writes=writes, is_dma=dma,
                      small=small)

    def TS(out, outb, in0, in0b, s1, op0, s2=None, op1=None, accum_out=None,
           accumb=None, extra=(), stream="vector"):
        reads = [in0b] + list(extra)
        writes = [outb] + ([accumb] if accumb is not None else [])
        try:
            sm_flag = out.free_size() <= 128 or accum_out is not None
        except Exception:
            sm_flag = True

        def f(eng):
            kw = dict(out=out, in0=in0, scalar1=s1, scalar2=s2, op0=op0)
            if op1 is not None:
                kw["op1"] = op1
            if accum_out is not None:
                kw["accum_out"] = accum_out
            return eng.tensor_scalar(**kw)
        return O(stream, f, reads=reads, writes=writes, small=sm_flag)

    O("sync", lambda e: e.dma_start(out=idb[:], in_=idb_i[:]),
      writes=[idb_b], dma=True)
    O("sync", lambda e: e.dma_start(out=idf[:], in_=idf_i[:]),
      writes=[idf_b], dma=True)
    O("sync", lambda e: e.dma_start(out=iou[:], in_=iou_i[:]),
      writes=[iou_b], dma=True)
    O("gpsimd", lambda e: e.dma_start(out=bkr_s[:], in_=bkr_i[:]),
      writes=[bkr_b], dma=True)
    O("gpsimd", lambda e: e.dma_start(out=bq_s[:], in_=bq_i[:]),
      writes=[bq_b], dma=True)
    O("gpsimd", lambda e: e.dma_start(out=bvp_s[:], in_=bvp_i[:]),
      writes=[bvp_b], dma=True)
    O("gpsimd", lambda e: e.dma_start(out=bo_s[:], in_=bo_i[:]),
      writes=[bo_b], dma=True)
    O("vector", lambda e: e.memset(ones1b[:], 1.0), writes=[ones1b_b])
    O("vector", lambda e: e.memset(onescb[:], 1.0), writes=[onescb_b], small=True)
    O("vector", lambda e: e.memset(zero_t[:], 0.0), writes=[zero_b])
    O("vector", lambda e: e.memset(eps_s[:], 1e-5), writes=[eps_b], small=True)
    _breg = {}

    def f_breg(eng):
        _breg["r"] = eng.to_reg(SEQ * NH - 1)
        return None
    O("gpsimd", f_breg)

    def quant_core(xu, xu_b, ta, tab, tb, tbb, tc, tcb, dst, dstb,
                   shb, shb_b, bcast_blocks=None, stream="vector"):
        def f_abs(eng):
            return eng.activation(out=ta, in_=xu, func=AF.Abs)
        O("scalar", f_abs, reads=[xu_b], writes=[tab])

        def f_m(eng):
            return eng.tensor_scalar(out=tc.bitcast(I32),
                                     in0=xu.bitcast(I32),
                                     scalar1=0x7F800000, scalar2=None,
                                     op0=OP.bitwise_and)
        O(stream, f_m, reads=[xu_b], writes=[tcb])
        TS(tc, tcb, tc, tcb, 2.0, OP.max, 6291456.0, OP.mult,
           stream=stream)

        def f_ad(eng):
            return eng.tensor_tensor(out=tb, in0=ta, in1=tc, op=OP.add)
        O(stream, f_ad, reads=[tab, tcb], writes=[tbb])

        def f_sb(eng):
            return eng.tensor_tensor(out=tb, in0=tb, in1=tc,
                                     op=OP.subtract)
        O(stream, f_sb, reads=[tbb, tcb], writes=[tbb])

        def f_sg(eng):
            return eng.activation(out=ta, in_=xu, func=AF.Sign)
        O("scalar", f_sg, reads=[xu_b], writes=[tab])

        def f_sm(eng):
            return eng.scalar_tensor_tensor(out=tb, in0=tb, scalar=12.0,
                                            in1=ta, op0=OP.min,
                                            op1=OP.mult)
        O(stream, f_sm, reads=[tbb, tab], writes=[tbb])
        if bcast_blocks is not None:
            nb = bcast_blocks
            shbb = shb.unsqueeze(2).broadcast_to([128, nb, 16])

            def f_dq(eng):
                return eng.tensor_tensor(
                    out=dst.rearrange("p (b k) -> p b k", k=16),
                    in0=tb.rearrange("p (b k) -> p b k", k=16),
                    in1=shbb, op=OP.mult)
        else:
            def f_dq(eng):
                return eng.tensor_tensor(out=dst, in0=tb, in1=shb,
                                         op=OP.mult)
        O(stream, f_dq, reads=[tbb, shb_b], writes=[dstb])

    def quant_tm(src, srcb, dst, dstb, Fdim, ta, tab, tb, tbb, tc, tcb,
                 xu, xu_b, alloc):
        NB = Fdim // 16
        bm, bm_b = alloc([128, NB], F32, "qbm")
        shb, shb_b = alloc([128, NB], BF16, "qshb")
        r2, r2_b = alloc([128, NB], F32, "qr2")
        s3 = src.rearrange("p (b k) -> p b k", k=16)

        def f_red(eng):
            return eng.tensor_reduce(out=bm[:], in_=s3, axis=AX.X,
                                     op=OP.max, apply_absolute_value=True)
        O("vector", f_red, reads=[srcb], writes=[bm_b], small=True)
        TS(bm[:], bm_b, bm[:], bm_b, 1.0 / 12.0, OP.mult, 0.5e-12, OP.max)
        O("vector", lambda e: e.reciprocal(out=r2[:], in_=bm[:]),
          reads=[bm_b], writes=[r2_b], small=True)
        O("vector", lambda e: e.tensor_copy(out=shb[:], in_=bm[:]),
          reads=[bm_b], writes=[shb_b], small=True)
        r2bc = r2.unsqueeze(2).broadcast_to([128, NB, 16])

        def f_xu(eng):
            return eng.tensor_tensor(
                out=xu.rearrange("p (b k) -> p b k", k=16), in0=s3,
                in1=r2bc, op=OP.mult)
        O("vector", f_xu, reads=[srcb, r2_b], writes=[xu_b])
        quant_core(xu, xu_b, ta, tab, tb, tbb, tc, tcb, dst, dstb,
                   shb[:], shb_b, bcast_blocks=NB)

    def ln_scratch(alloc):
        sm, sm_b = alloc([128, 4], F32, "ln_s")
        dmp, dmp_b = alloc([128, D], F32, "ln_d")
        return (sm, sm_b, dmp, dmp_b)

    def emit_ln(xt, xtb, xln, xln_b, sc):
        sm, sm_b, dmp, dmp_b = sc

        def f_sum(eng):
            return eng.activation(out=dmp[:], in_=xt, func=AF.Copy,
                                  accum_out=sm[:, 3:4])
        O("scalar", f_sum, reads=[xtb], writes=[dmp_b, sm_b], small=True)

        def f_sq(eng):
            return eng.activation(out=dmp[:], in_=xt, func=AF.Square,
                                  accum_out=sm[:, 2:3])
        O("scalar", f_sq, reads=[xtb], writes=[dmp_b, sm_b], small=True)
        TS(sm[:, 0:1], sm_b, sm[:, 3:4], sm_b, -1.0 / D, OP.mult)

        def f_m2(eng):
            return eng.activation(out=sm[:, 1:2], in_=sm[:, 0:1],
                                  func=AF.Square)
        O("scalar", f_m2, reads=[sm_b], writes=[sm_b], small=True)

        def f_var(eng):
            return eng.scalar_tensor_tensor(out=sm[:, 1:2], in0=sm[:, 2:3],
                                            scalar=1.0 / D, in1=sm[:, 1:2],
                                            op0=OP.mult, op1=OP.subtract)
        O("vector", f_var, reads=[sm_b], writes=[sm_b], small=True)

        def f_sqrt(eng):
            return eng.activation(out=sm[:, 1:2], in_=sm[:, 1:2],
                                  func=AF.Sqrt, bias=eps_s[:])
        O("scalar", f_sqrt, reads=[sm_b, eps_b], writes=[sm_b], small=True)
        O("vector", lambda e: e.reciprocal(out=sm[:, 1:2], in_=sm[:, 1:2]),
          reads=[sm_b], writes=[sm_b], small=True)

        def f_app(eng):
            return eng.tensor_scalar(out=xln, in0=xt, scalar1=sm[:, 0:1],
                                     scalar2=sm[:, 1:2], op0=OP.add,
                                     op1=OP.mult)
        O("vector", f_app, reads=[xtb, sm_b], writes=[xln_b])

    def emit_t8(src, srcb, dst_fn, dstb):
        pt, pt_b = psb(6)
        for kc in range(8):
            half = kc % 2

            def f_tr(eng, kc=kc, half=half):
                return eng.transpose(pt[:, half * 128:(half + 1) * 128],
                                     src[:, kc * 128:(kc + 1) * 128],
                                     idb[:])
            O("tensor", f_tr, reads=[srcb, idb_b], writes=[pt_b])

            def f_ev(eng, kc=kc, half=half):
                return eng.tensor_copy(out=dst_fn(kc),
                                       in_=pt[:, half * 128:(half + 1) * 128])
            O("vector", f_ev, reads=[pt_b], writes=[dstb])

    pmark = AR.mark()
    for (w_in, wq_dst, wq_dbs) in ((wg_i, wgq_d, wgq_db),
                                   (wu_i, wuq_d, wuq_db)):
        for m in range(24):
            AR.reset(pmark)
            wt, wt_b = AR.alloc([128, D], F32, "pw")
            wq_t, wq_b = AR.alloc([128, D], BF16, "pwq")
            stg, stg_b = AR.alloc([128, 8, 128], BF16, "pst")
            ta, tab = AR.alloc([128, D], F32, "pqa")
            tbx, tbb = AR.alloc([128, D], F32, "pqb")
            tcx, tcb = AR.alloc([128, D], F32, "pqc")
            xu, xu_b = AR.alloc([128, D], F32, "pqx")

            def f_ld(eng, m=m, w_in=w_in, wt=wt):
                return eng.dma_start(out=wt[:],
                                     in_=w_in[m * 128:(m + 1) * 128, :])
            O("sync", f_ld, writes=[wt_b], dma=True)
            quant_tm(wt[:], wt_b, wq_t[:], wq_b, D, ta[:], tab, tbx[:], tbb,
                     tcx[:], tcb, xu[:], xu_b, AR.alloc)
            emit_t8(wq_t, wq_b, lambda kc, stg=stg: stg[:, kc, :], stg_b)

            def f_st(eng, m=m, wq_dst=wq_dst, stg=stg):
                return eng.dma_start(out=wq_dst[m], in_=stg[:])
            O("sync", f_st, reads=[stg_b], writes=[wq_dbs[m]], dma=True)

    for t in range(8):
        AR.reset(pmark)
        wt, wt_b = AR.alloc([128, I_FF], F32, "pwd")
        wq_t, wq_b = AR.alloc([128, I_FF], BF16, "pwdq")
        stg, stg_b = AR.alloc([128, 128], BF16, "pwds")
        ta, tab = AR.alloc([128, I_FF], F32, "pda")
        tbx, tbb = AR.alloc([128, I_FF], F32, "pdb")
        tcx, tcb = AR.alloc([128, I_FF], F32, "pdc")
        xu, xu_b = AR.alloc([128, I_FF], F32, "pdx")

        def f_ld(eng, t=t, wt=wt):
            return eng.dma_start(out=wt[:],
                                 in_=wd_i[t * 128:(t + 1) * 128, :])
        O("sync", f_ld, writes=[wt_b], dma=True)
        quant_tm(wt[:], wt_b, wq_t[:], wq_b, I_FF, ta[:], tab, tbx[:], tbb,
                 tcx[:], tcb, xu[:], xu_b, AR.alloc)
        for c in range(24):
            pt, pt_b = psb(6)
            if c < 16:
                view = wq_t.rearrange("p (b j) -> p j b", j=16)[:, c, 0:128]

                def f_tr(eng, view=view, pt=pt):
                    return eng.transpose(pt[:, 0:128], view, idb[:])
                O("tensor", f_tr, reads=[wq_b, idb_b], writes=[pt_b])

                def f_ev(eng, stg=stg, pt=pt):
                    return eng.tensor_copy(out=stg[:], in_=pt[:, 0:128])
                O("vector", f_ev, reads=[pt_b], writes=[stg_b])
            else:
                jj = c - 16
                O("vector", lambda e, stg=stg: e.memset(stg[:], 0.0),
                  writes=[stg_b])
                vtail = wq_t[:, 2048:2816].rearrange(
                    "p (b j) -> p j b", j=16)
                v0 = vtail[:, 2 * jj, :]
                v1 = vtail[:, 2 * jj + 1, :]

                def f_t0(eng, v0=v0, pt=pt):
                    return eng.transpose(pt[0:48, 0:128], v0, idb[:])
                O("tensor", f_t0, reads=[wq_b, idb_b], writes=[pt_b])

                def f_e0(eng, stg=stg, pt=pt):
                    return eng.tensor_copy(out=stg[0:48, :],
                                           in_=pt[0:48, 0:128])
                O("vector", f_e0, reads=[pt_b], writes=[stg_b])

                def f_t1(eng, v1=v1, pt=pt):
                    return eng.transpose(pt[0:48, 128:256], v1, idb[:])
                O("tensor", f_t1, reads=[wq_b, idb_b], writes=[pt_b])

                def f_e1(eng, stg=stg, pt=pt):
                    return eng.tensor_copy(out=stg[64:112, :],
                                           in_=pt[0:48, 128:256])
                O("vector", f_e1, reads=[pt_b], writes=[stg_b])

            def f_st(eng, c=c, t=t, stg=stg):
                return eng.dma_start(out=wdq_d[c, :, t * 128:(t + 1) * 128],
                                     in_=stg[:])
            O("sync", f_st, reads=[stg_b], writes=[wdq_db[c]], dma=True)

    def emit_ffn(seq, first):
        base = 0
        for g in range(NTG):
            AR.reset(base)
            toks = [g * 4 + i for i in range(4)]
            hf, hf_b = AR.alloc([128, 24, TG], F32, "hf")
            hq, hq_b = AR.alloc([128, 24, TG], BF16, "hq")
            xqf, xqf_b = AR.alloc([128, 8, TG], BF16, "xqf")
            xts = [AR.alloc([128, D], F32, f"xt{i}") for i in range(4)]
            sA, sA_b = AR.alloc([128, TG], F32, "sA")
            sB, sB_b = AR.alloc([128, TG], F32, "sB")
            shAb, shAb_b = AR.alloc([128, TG], BF16, "shAb")
            r2A, r2A_b = AR.alloc([128, TG], F32, "r2A")
            shBb, shBb_b = AR.alloc([64, TG], BF16, "shBb")
            r2B, r2B_b = AR.alloc([64, TG], F32, "r2B")
            ta, tab = AR.alloc([128, D], F32, "qta")
            tbx, tbb = AR.alloc([128, D], F32, "qtb")
            tcx, tcb = AR.alloc([128, D], F32, "qtc")
            xu, xu_b = AR.alloc([128, D], F32, "qxu")
            sg_t, sg_b = AR.alloc([128, TG], F32, "sg")
            sc = ln_scratch(AR.alloc)
            xln, xln_b = AR.alloc([128, D], F32, "xln")
            xq_tm, xq_tm_b = AR.alloc([128, D], BF16, "xqtm")
            qmark = AR.mark()

            for i, ti in enumerate(toks):
                AR.reset(qmark)
                xt, xtb = xts[i]
                src_ap = (x_in[seq * SEQ + ti * 128:
                               seq * SEQ + (ti + 1) * 128, :]
                          if first else
                          x_d[ti * 128:(ti + 1) * 128, :])

                def f_lx(eng, xt=xt, src_ap=src_ap):
                    return eng.dma_start(out=xt[:], in_=src_ap)
                rd = [] if first else [x_db[ti]]
                O("sync", f_lx, reads=rd, writes=[xtb], dma=True)
                emit_ln(xt[:], xtb, xln[:], xln_b, sc)
                quant_tm(xln[:], xln_b, xq_tm[:], xq_tm_b, D, ta[:], tab,
                         tbx[:], tbb, tcx[:], tcb, xu[:], xu_b, AR.alloc)
                emit_t8(xq_tm, xq_tm_b,
                        lambda kc, i=i: xqf[:, kc, i * 128:(i + 1) * 128],
                        xqf_b)

            wmark = AR.mark()
            wg_t = []
            for db2 in range(2):
                a0, b0 = AR.alloc([128, 8, 128], BF16, f"wgt{db2}")
                a1, b1 = AR.alloc([128, 8, 128], BF16, f"wut{db2}")
                wg_t.append(((a0, b0), (a1, b1)))
            for m in range(24):
                (gA, gAb), (uA, uAb) = wg_t[m % 2]

                def f_lg(eng, m=m, gA=gA):
                    return eng.dma_start(out=gA[:], in_=wgq_d[m])
                O("sync", f_lg, reads=[wgq_db[m]], writes=[gAb], dma=True)

                def f_lu(eng, m=m, uA=uA):
                    return eng.dma_start(out=uA[:], in_=wuq_d[m])
                O("sync", f_lu, reads=[wuq_db[m]], writes=[uAb], dma=True)
                pg, pg_b = psf(0 if m % 2 == 0 else 1)
                pu, pu_b = psf(2 if m % 2 == 0 else 3)
                for kc in range(8):
                    def f_mg(eng, kc=kc, gA=gA, pg=pg):
                        return eng.matmul(pg[:], gA[:, kc, :], xqf[:, kc, :],
                                          start=(kc == 0), stop=(kc == 7))
                    O("tensor", f_mg, reads=[gAb, xqf_b], writes=[pg_b])
                for kc in range(8):
                    def f_mu(eng, kc=kc, uA=uA, pu=pu):
                        return eng.matmul(pu[:], uA[:, kc, :], xqf[:, kc, :],
                                          start=(kc == 0), stop=(kc == 7))
                    O("tensor", f_mu, reads=[uAb, xqf_b], writes=[pu_b])

                def f_sil(eng, pg=pg):
                    return eng.activation(out=sg_t[:], in_=pg[:],
                                          func=AF.Silu)
                O("scalar", f_sil, reads=[pg_b], writes=[sg_b])

                def f_h(eng, m=m, pu=pu):
                    return eng.scalar_tensor_tensor(
                        out=hf[:, m, :], in0=pu[:], scalar=1.0, in1=sg_t[:],
                        op0=OP.mult, op1=OP.mult)
                O("vector", f_h, reads=[pu_b, sg_b], writes=[hf_b])

                def f_abs(eng, m=m):
                    return eng.activation(out=ta[:, 0:TG], in_=hf[:, m, :],
                                          func=AF.Abs)
                O("scalar", f_abs, reads=[hf_b], writes=[tab])
                tgt, tgtb = (sA, sA_b) if m < 16 else (sB, sB_b)
                if m in (0, 16):
                    O("vector",
                      lambda e, tgt=tgt: e.tensor_copy(out=tgt[:],
                                                       in_=ta[:, 0:TG]),
                      reads=[tab], writes=[tgtb])
                else:
                    def f_mx(eng, tgt=tgt):
                        return eng.tensor_tensor(out=tgt[:], in0=tgt[:],
                                                 in1=ta[:, 0:TG], op=OP.max)
                    O("vector", f_mx, reads=[tab, tgtb], writes=[tgtb])

            def f_foldc(eng):
                return eng.tensor_copy(out=ta[0:64, 0:TG],
                                       in_=sB[64:128, :])
            O("vector", f_foldc, reads=[sB_b], writes=[tab])

            def f_fold(eng):
                return eng.tensor_tensor(out=sB[0:64, :], in0=sB[0:64, :],
                                         in1=ta[0:64, 0:TG], op=OP.max)
            O("vector", f_fold, reads=[sB_b, tab], writes=[sB_b])
            TS(sA[:], sA_b, sA[:], sA_b, 1.0 / 12.0, OP.mult, 0.5e-12,
               OP.max)
            TS(sB[0:64, :], sB_b, sB[0:64, :], sB_b, 1.0 / 12.0, OP.mult,
               0.5e-12, OP.max)
            O("vector", lambda e: e.reciprocal(out=r2A[:], in_=sA[:]),
              reads=[sA_b], writes=[r2A_b])
            O("vector", lambda e: e.reciprocal(out=r2B[:], in_=sB[0:64, :]),
              reads=[sB_b], writes=[r2B_b])
            O("vector", lambda e: e.tensor_copy(out=shAb[:], in_=sA[:]),
              reads=[sA_b], writes=[shAb_b])
            O("vector",
              lambda e: e.tensor_copy(out=shBb[:], in_=sB[0:64, :]),
              reads=[sB_b], writes=[shBb_b])

            hcp, hcp_b = AR.alloc([64, TG], F32, "hcp")
            for m in range(24):
                strm = "vector"
                qta, qtab = ta[:, 0:TG], tab
                qtb, qtbb = tbx[:, 0:TG], tbb
                qtc, qtcb = tcx[:, 0:TG], tcb
                qxu, qxub = xu[:, 0:TG], xu_b
                qcp, qcpb = hcp, hcp_b
                if m < 16:
                    def f_xu(eng, m=m, qxu=qxu):
                        return eng.tensor_tensor(out=qxu,
                                                 in0=hf[:, m, :],
                                                 in1=r2A[:], op=OP.mult)
                    O(strm, f_xu, reads=[hf_b, r2A_b], writes=[qxub])
                    quant_core(qxu, qxub, qta, qtab, qtb, qtbb, qtc, qtcb,
                               hq[:, m, :], hq_b, shAb[:], shAb_b,
                               bcast_blocks=None, stream=strm)
                else:
                    for lohi in range(2):
                        if lohi == 0:
                            def f_xu(eng, m=m, qxu=qxu):
                                return eng.tensor_tensor(
                                    out=qxu[0:64, :],
                                    in0=hf[:, m, :][0:64, :],
                                    in1=r2B[:], op=OP.mult)
                            O(strm, f_xu, reads=[hf_b, r2B_b],
                              writes=[qxub])
                        else:
                            def f_cp(eng, m=m, qcp=qcp):
                                return eng.tensor_copy(
                                    out=qcp[:], in_=hf[:, m, :][64:128, :])
                            O(strm, f_cp, reads=[hf_b], writes=[qcpb])

                            def f_xu(eng, qxu=qxu, qcp=qcp):
                                return eng.tensor_tensor(
                                    out=qxu[0:64, :], in0=qcp[:],
                                    in1=r2B[:], op=OP.mult)
                            O(strm, f_xu, reads=[qcpb, r2B_b],
                              writes=[qxub])
                        dsl = slice(0, 64) if lohi == 0 else slice(64, 128)
                        quant_core(qxu[0:64, :], qxub, qta[0:64, :],
                                   qtab, qtb[0:64, :], qtbb,
                                   qtc[0:64, :], qtcb,
                                   hq[:, m, :][dsl, :], hq_b, shBb[:],
                                   shBb_b, bcast_blocks=None, stream=strm)

            wdt = []
            for db2 in range(2):
                a0, b0 = AR.alloc([128, D], BF16, f"wdt{db2}")
                wdt.append((a0, b0))
            for kc in range(24):
                wda, wdab = wdt[kc % 2]

                def f_lw(eng, kc=kc, wda=wda):
                    return eng.dma_start(out=wda[:], in_=wdq_d[kc])
                O("sync", f_lw, reads=[wdq_db[kc]], writes=[wdab], dma=True)
                for mt in range(4):
                    for nn in range(2):
                        pd, pd_b = psf(mt * 2 + nn)

                        def f_md(eng, kc=kc, mt=mt, nn=nn, wda=wda, pd=pd):
                            return eng.matmul(
                                pd[:], hq[:, kc, mt * 128:(mt + 1) * 128],
                                wda[:, nn * 512:(nn + 1) * 512],
                                start=(kc == 0), stop=(kc == 23))
                        O("tensor", f_md, reads=[hq_b, wdab], writes=[pd_b])
            for mt in range(4):
                ti = toks[mt]
                xt, xtb = xts[mt]
                for nn in range(2):
                    pd, pd_b = psf(mt * 2 + nn)

                    def f_res(eng, nn=nn, pd=pd, xt=xt):
                        return eng.scalar_tensor_tensor(
                            out=xt[:, nn * 512:(nn + 1) * 512], in0=pd[:],
                            scalar=0.5, in1=xt[:, nn * 512:(nn + 1) * 512],
                            op0=OP.mult, op1=OP.add)
                    O("vector", f_res, reads=[pd_b, xtb], writes=[xtb])

                def f_sx(eng, ti=ti, xt=xt):
                    return eng.dma_start(out=x_d[ti * 128:(ti + 1) * 128, :],
                                         in_=xt[:])
                O("sync", f_sx, reads=[xtb], writes=[x_db[ti]], dma=True)

    def emit_attn(seq):
        AR.reset(0)
        norms2, norms2_b = AR.alloc([16, SEQ], F32, "nrm2")
        cums, cums_b = AR.alloc([16, SEQ], F32, "cums")
        bdmp, bdmp_b = AR.alloc([16, SEQ], F32, "bdmp")
        zz16, zz16_b = AR.alloc([16, SEQ], F32, "zz16")
        srcu, srcu_b = AR.alloc([128, 4], U32, "srcu")
        ct_t, ct_b = AR.alloc([128, 16, 16], F32, "ct")
        bs, bs_b = AR.alloc([16, 8], F32, "bis")
        v_sb, v_b = AR.alloc([128, 16, NH * 72], BF16, "v_sb")
        kfm, kfm_b = AR.alloc([128, 8], F32, "kfm")
        kf2, kf2_b = AR.alloc([128, 8], F32, "kf2")
        nbh, nbh_b = AR.alloc([1, 16], F32, "nbh")
        sq16, sq16_b = AR.alloc([1, 16], F32, "sq16")
        km16, km16_b = AR.alloc([1, 16], F32, "km16")
        kdr, kdr_b = AR.alloc([1, 64], F32, "kdr")
        markA = AR.mark()

        xb, xb_b = AR.alloc([128, 8, SEQ], BF16, "xb")
        stg, stg_b = AR.alloc([128, SEQ], BF16, "astg")
        kdu, kdu_b = AR.alloc([128, SEQ], BF16, "kdu")
        qf32, qf32_b = AR.alloc([128, 512], F32, "qf32")
        sqt, sqt_b = AR.alloc([128, D], F32, "asqt")
        nrm_tm, nrm_tm_b = AR.alloc([128, NH], F32, "anrm")
        sc = ln_scratch(AR.alloc)
        xt_a, xt_ab = AR.alloc([128, D], F32, "axt")
        xlnb, xlnb_b = AR.alloc([128, D], BF16, "axln")
        wq_s, wq_sb = AR.alloc([128, 8, D], BF16, "wq_s")

        for ti in range(16):
            def f_lx(eng, ti=ti):
                return eng.dma_start(out=xt_a[:],
                                     in_=x_d[ti * 128:(ti + 1) * 128, :])
            O("sync", f_lx, reads=[x_db[ti]], writes=[xt_ab], dma=True)
            emit_ln(xt_a[:], xt_ab, xlnb[:], xlnb_b, sc)
            emit_t8(xlnb, xlnb_b,
                    lambda kc, ti=ti: xb[:, kc, ti * 128:(ti + 1) * 128],
                    xb_b)

        def f_lwq(eng):
            return eng.dma_start(
                out=wq_s[:], in_=wqt_i.rearrange("(kc p) n -> p kc n",
                                                 p=128))
        O("gpsimd", f_lwq, writes=[wq_sb], dma=True)
        for mt in range(16):
            for nn in range(2):
                pq, pq_b = psf(nn)
                for kc in range(8):
                    def f_m(eng, kc=kc, mt=mt, nn=nn, pq=pq):
                        return eng.matmul(
                            pq[:], xb[:, kc, mt * 128:(mt + 1) * 128],
                            wq_s[:, kc, nn * 512:(nn + 1) * 512],
                            start=(kc == 0), stop=False)
                    O("tensor", f_m, reads=[xb_b, wq_sb], writes=[pq_b])

                def f_b(eng, nn=nn, pq=pq):
                    return eng.matmul(pq[:], ones1b[0:1, 0:128],
                                      bq_s[:, nn * 512:(nn + 1) * 512],
                                      start=False, stop=True)
                O("tensor", f_b, reads=[ones1b_b, bq_b], writes=[pq_b])

                def f_ev1(eng, pq=pq):
                    return eng.activation(out=qf32[:], in_=pq[:],
                                          func=AF.Copy, scale=0.125)
                O("scalar", f_ev1, reads=[pq_b], writes=[qf32_b])

                def f_ev(eng, nn=nn):
                    return eng.activation(
                        out=stg[:, nn * 512:(nn + 1) * 512], in_=qf32[:],
                        func=AF.Copy)
                O("scalar", f_ev, reads=[qf32_b], writes=[stg_b])

                def f_sq2(eng, nn=nn, pq=pq):
                    return eng.activation(
                        out=sqt[:, nn * 512:(nn + 1) * 512], in_=pq[:],
                        func=AF.Square)
                O("scalar", f_sq2, reads=[pq_b], writes=[sqt_b])

            def f_red(eng):
                return eng.tensor_reduce(
                    out=nrm_tm[:],
                    in_=sqt.rearrange("p (h d) -> p h d", d=HD), axis=AX.X,
                    op=OP.add)
            O("vector", f_red, reads=[sqt_b], writes=[nrm_tm_b],
              small=True)
            pt7, pt7_b = psf(7)

            def f_trn(eng, pt7=pt7):
                return eng.transpose(pt7[0:16, 0:128], nrm_tm[:], idf[:])
            O("tensor", f_trn, reads=[nrm_tm_b, idf_b], writes=[pt7_b])

            def f_evn(eng, mt=mt, pt7=pt7):
                return eng.tensor_copy(
                    out=norms2[:, mt * 128:(mt + 1) * 128],
                    in_=pt7[0:16, 0:128])
            O("vector", f_evn, reads=[pt7_b], writes=[norms2_b], small=True)

            def f_stq(eng, mt=mt):
                return eng.dma_start(out=q_d[mt * 128:(mt + 1) * 128, :],
                                     in_=stg[:, 0:D])
            O("sync", f_stq, reads=[stg_b], writes=[q_db[mt]], dma=True)

        def f_lwk(eng):
            return eng.dma_start(
                out=wq_s[:], in_=wkt_i.rearrange("(kc p) n -> p kc n",
                                                 p=128))
        O("gpsimd", f_lwk, writes=[wq_sb], dma=True)
        for mo in range(8):
            for nn in range(4):
                pk, pk_b = psf(2 + (nn % 2))
                for kc in range(8):
                    def f_m(eng, kc=kc, mo=mo, nn=nn, pk=pk):
                        return eng.matmul(
                            pk[:], wq_s[:, kc, mo * 128:(mo + 1) * 128],
                            xb[:, kc, nn * 512:(nn + 1) * 512],
                            start=(kc == 0), stop=False)
                    O("tensor", f_m, reads=[xb_b, wq_sb], writes=[pk_b])

                def f_kb(eng, mo=mo, pk=pk):
                    return eng.matmul(pk[:],
                                      bkr_s[0:1, mo * 128:(mo + 1) * 128],
                                      ones1b[:], start=False, stop=True)
                O("tensor", f_kb, reads=[bkr_b, ones1b_b], writes=[pk_b])

                def f_ev(eng, nn=nn, pk=pk):
                    return eng.activation(
                        out=stg[:, nn * 512:(nn + 1) * 512], in_=pk[:],
                        func=AF.Copy)
                O("scalar", f_ev, reads=[pk_b], writes=[stg_b])
            for hh in range(2):
                def f_st(eng, mo=mo, hh=hh):
                    return eng.dma_start(out=k_d[mo * 2 + hh],
                                         in_=stg[hh * 64:(hh + 1) * 64, :])
                O("sync", f_st, reads=[stg_b], writes=[k_db[mo * 2 + hh]],
                  dma=True)
            TS(kdu[:], kdu_b, stg[:, 0:SEQ], stg_b, 1.0, OP.mult, -1e30,
               OP.max, accum_out=kfm[:, mo:mo + 1], accumb=kfm_b)
            TS(kdu[:], kdu_b, stg[:, 0:SEQ], stg_b, -1.0, OP.mult, -1e30,
               OP.max, accum_out=kf2[:, mo:mo + 1], accumb=kf2_b)

            def f_mg2(eng, mo=mo):
                return eng.tensor_tensor(out=kfm[:, mo:mo + 1],
                                         in0=kfm[:, mo:mo + 1],
                                         in1=kf2[:, mo:mo + 1], op=OP.max)
            O("vector", f_mg2, reads=[kfm_b, kf2_b], writes=[kfm_b],
              small=True)
            pt6k, pt6k_b = psf(6)

            def f_tk(eng, mo=mo, pt6k=pt6k):
                return eng.transpose(pt6k[0:1, 0:128], kfm[:, mo:mo + 1],
                                     idf[:])
            O("tensor", f_tk, reads=[kfm_b, idf_b], writes=[pt6k_b])
            TS(kdr[:], kdr_b, pt6k[0:1, 0:64], pt6k_b, 1.0, OP.mult,
               -1e30, OP.max, accum_out=km16[0:1, 2 * mo:2 * mo + 1],
               accumb=km16_b)
            TS(kdr[:], kdr_b, pt6k[0:1, 64:128], pt6k_b, 1.0, OP.mult,
               -1e30, OP.max, accum_out=km16[0:1, 2 * mo + 1:2 * mo + 2],
               accumb=km16_b)

        wv_s, wv_sb = AR.alloc([128, 8, NH * 72], BF16, "wv_s")

        def f_lwv(eng):
            return eng.dma_start(
                out=wv_s[:], in_=wvtp_i.rearrange("(kc p) n -> p kc n",
                                                  p=128))
        O("gpsimd", f_lwv, writes=[wv_sb], dma=True)
        for mt in range(16):
            for nn in range(3):
                n0 = nn * 512
                n1 = min((nn + 1) * 512, NH * 72)
                pv, pv_b = psf(4 + (nn % 2))
                for kc in range(8):
                    def f_m(eng, kc=kc, mt=mt, n0=n0, n1=n1, pv=pv):
                        return eng.matmul(
                            pv[:, 0:n1 - n0],
                            xb[:, kc, mt * 128:(mt + 1) * 128],
                            wv_s[:, kc, n0:n1], start=(kc == 0), stop=False)
                    O("tensor", f_m, reads=[xb_b, wv_sb], writes=[pv_b])

                def f_b(eng, n0=n0, n1=n1, pv=pv):
                    return eng.matmul(pv[:, 0:n1 - n0], ones1b[0:1, 0:128],
                                      bvp_s[:, n0:n1], start=False,
                                      stop=True)
                O("tensor", f_b, reads=[ones1b_b, bvp_b], writes=[pv_b])

                def f_ev(eng, mt=mt, n0=n0, n1=n1, pv=pv):
                    return eng.activation(out=v_sb[:, mt, n0:n1],
                                          in_=pv[:, 0:n1 - n0],
                                          func=AF.Copy)
                O("scalar", f_ev, reads=[pv_b], writes=[v_b])

        lo_t, hi_t, mid_t = bs[:, 0:1], bs[:, 1:2], bs[:, 2:3]
        cnt_t, msk_t, th_t = bs[:, 3:4], bs[:, 4:5], bs[:, 5:6]
        O("vector", lambda e: e.memset(zz16[:], 0.0), writes=[zz16_b])
        O("vector", lambda e: e.memset(bs[:, 0:1], 0.0), writes=[bs_b], small=True)

        def f_mx0(eng):
            return eng.tensor_reduce(out=bs[:, 1:2], in_=norms2[:],
                                     axis=AX.X, op=OP.max)
        O("vector", f_mx0, reads=[norms2_b], writes=[bs_b], small=True)
        TS(hi_t, bs_b, hi_t, bs_b, 1.0001, OP.mult, 1e-6, OP.add)
        O("vector", lambda e: e.tensor_copy(out=bs[:, 7:8], in_=hi_t),
          reads=[bs_b], writes=[bs_b], small=True)
        for it in range(BISECT_ITERS):
            def f_mid(eng):
                return eng.tensor_tensor(out=mid_t, in0=lo_t, in1=hi_t,
                                         op=OP.add)
            O("vector", f_mid, reads=[bs_b], writes=[bs_b], small=True)
            TS(mid_t, bs_b, mid_t, bs_b, 0.5, OP.mult)
            TS(bdmp[:], bdmp_b, norms2[:], norms2_b, mid_t, OP.is_gt, 0.0,
               OP.add, accum_out=cnt_t, accumb=bs_b, extra=[bs_b])
            TS(msk_t, bs_b, cnt_t, bs_b, float(TOPU), OP.is_ge)
            tmp_a, tmp_b2 = bs[:, 6:7], bs_b

            def f_d1(eng):
                return eng.tensor_tensor(out=tmp_a, in0=mid_t, in1=lo_t,
                                         op=OP.subtract)
            O("vector", f_d1, reads=[bs_b], writes=[bs_b], small=True)

            def f_d2(eng):
                return eng.tensor_tensor(out=tmp_a, in0=tmp_a, in1=msk_t,
                                         op=OP.mult)
            O("vector", f_d2, reads=[bs_b], writes=[bs_b], small=True)

            def f_d3(eng):
                return eng.tensor_tensor(out=lo_t, in0=lo_t, in1=tmp_a,
                                         op=OP.add)
            O("vector", f_d3, reads=[bs_b], writes=[bs_b], small=True)
            TS(msk_t, bs_b, msk_t, bs_b, -1.0, OP.mult, 1.0, OP.add)

            def f_e1(eng):
                return eng.tensor_tensor(out=tmp_a, in0=mid_t, in1=hi_t,
                                         op=OP.subtract)
            O("vector", f_e1, reads=[bs_b], writes=[bs_b], small=True)

            def f_e2(eng):
                return eng.tensor_tensor(out=tmp_a, in0=tmp_a, in1=msk_t,
                                         op=OP.mult)
            O("vector", f_e2, reads=[bs_b], writes=[bs_b], small=True)

            def f_e3(eng):
                return eng.tensor_tensor(out=hi_t, in0=hi_t, in1=tmp_a,
                                         op=OP.add)
            O("vector", f_e3, reads=[bs_b], writes=[bs_b], small=True)

        def f_mask(eng):
            return eng.scalar_tensor_tensor(out=bdmp[:], in0=norms2[:],
                                            scalar=hi_t, in1=norms2[:],
                                            op0=OP.is_le, op1=OP.mult)
        O("vector", f_mask, reads=[norms2_b, bs_b], writes=[bdmp_b])

        def f_th(eng):
            return eng.tensor_reduce(out=th_t, in_=bdmp[:], axis=AX.X,
                                     op=OP.max)
        O("vector", f_th, reads=[bdmp_b], writes=[bs_b], small=True)
        TS(bdmp[:], bdmp_b, norms2[:], norms2_b, th_t, OP.is_ge,
           extra=[bs_b])

        def f_scan(eng):
            return eng.tensor_tensor_scan(out=cums[:], data0=bdmp[:],
                                          data1=zz16[:], initial=0.0,
                                          op0=OP.add, op1=OP.add)
        O("vector", f_scan, reads=[bdmp_b, zz16_b], writes=[cums_b])

        for lc in range(16):
            pt7, pt7_b = psf(7)

            def f_tr(eng, lc=lc, pt7=pt7):
                return eng.transpose(pt7[0:128, 0:16],
                                     cums[:, lc * 128:(lc + 1) * 128],
                                     idf[0:16, 0:16])
            O("tensor", f_tr, reads=[cums_b, idf_b], writes=[pt7_b])

            def f_ev(eng, lc=lc, pt7=pt7):
                return eng.tensor_copy(out=ct_t[:, lc, :],
                                       in_=pt7[0:128, 0:16])
            O("vector", f_ev, reads=[pt7_b], writes=[ct_b], small=True)

        for ti in range(16):
            def f_z(eng, ti=ti):
                return eng.dma_start(out=of_d[ti * 128:(ti + 1) * 128, :],
                                     in_=zero_t[:])
            O("sync", f_z, reads=[zero_b], writes=[of_zb[ti]], dma=True)

        AR.reset(markA)
        smat, smat_b = AR.alloc([128, U], BF16, "smat")
        srcf, srcf_b = AR.alloc([1, U], F32, "srcf")
        kp = []
        for db2 in range(2):
            a0, b0 = AR.alloc([65, SEQ], BF16, f"kp{db2}")
            O("vector", lambda e, a0=a0: e.memset(a0[64:65, :], 1.0),
              writes=[b0])
            kp.append((a0, b0))
        qsel, qsel_b = AR.alloc([128, 4, HD], BF16, "qsel")
        qst, qst_b = AR.alloc([65, U], BF16, "qst")
        att, att_b = AR.alloc([128, 16, U], BF16, "att")
        ostg, ostg_b = AR.alloc([128, HD], BF16, "ostg")
        mxs, mxs_b = AR.alloc([128, 4], F32, "mxs")

        pt7, pt7_b = psf(7)

        def f_tn2(eng, pt7=pt7):
            return eng.transpose(pt7[0:1, 0:16], bs[0:16, 7:8],
                                 idf[0:16, 0:16])
        O("tensor", f_tn2, reads=[bs_b, idf_b], writes=[pt7_b])

        def f_en2(eng, pt7=pt7):
            return eng.tensor_copy(out=sq16[:], in_=pt7[0:1, 0:16])
        O("vector", f_en2, reads=[pt7_b], writes=[sq16_b], small=True)

        def f_sqr(eng):
            return eng.activation(out=sq16[:], in_=sq16[:], func=AF.Sqrt)
        O("scalar", f_sqr, reads=[sq16_b], writes=[sq16_b], small=True)

        def f_nb(eng):
            return eng.tensor_tensor(out=nbh[:], in0=sq16[:], in1=km16[:],
                                     op=OP.mult)
        O("vector", f_nb, reads=[sq16_b, km16_b], writes=[nbh_b],
          small=True)
        TS(nbh[:], nbh_b, nbh[:], nbh_b, -1.0, OP.mult)

        for h in range(NH):
            kpa, kpa_b = kp[h % 2]
            pt7, pt7_b = psf(7)
            for lc in range(16):
                def f_s(eng, lc=lc, h=h):
                    return eng.tensor_scalar(
                        out=smat[:], in0=iou[:],
                        scalar1=ct_t[:, lc, h:h + 1], scalar2=None,
                        op0=OP.is_ge)
                O("vector", f_s, reads=[iou_b, ct_b], writes=[smat_b])

                def f_mm(eng, lc=lc, pt7=pt7):
                    return eng.matmul(pt7[0:1, 0:U], onescb[:], smat[:],
                                      start=(lc == 0), stop=(lc == 15))
                O("tensor", f_mm, reads=[onescb_b, smat_b], writes=[pt7_b])

            def f_evs(eng, pt7=pt7):
                return eng.tensor_copy(out=srcf[:], in_=pt7[0:1, 0:U])
            O("vector", f_evs, reads=[pt7_b], writes=[srcf_b],
              small=True)
            TS(srcf[:], srcf_b, srcf[:], srcf_b, 16.0, OP.mult, float(h),
               OP.add)
            for uc in range(4):
                pt6, pt6_b = psf(6)

                def f_tr(eng, uc=uc, pt6=pt6):
                    return eng.transpose(pt6[0:128, 0:1],
                                         srcf[:, uc * 128:(uc + 1) * 128],
                                         idf[0:1, 0:1])
                O("tensor", f_tr, reads=[srcf_b, idf_b], writes=[pt6_b])

                def f_cv(eng, uc=uc, pt6=pt6):
                    return eng.tensor_copy(out=srcu[:, uc:uc + 1],
                                           in_=pt6[0:128, 0:1])
                O("vector", f_cv, reads=[pt6_b], writes=[srcu_b], small=True)

            def f_lk(eng, h=h, kpa=kpa):
                return eng.dma_start(out=kpa[0:64, :], in_=k_d[h])
            O("sync", f_lk, reads=[k_db[h]], writes=[kpa_b], dma=True)
            O("vector", lambda e: e.memset(qsel[:], 0.0), writes=[qsel_b])
            for uc in range(4):
                def f_g(eng, h=h, uc=uc):
                    return eng.indirect_dma_start(
                        out=qsel[:, uc, :], out_offset=None,
                        in_=q_d.rearrange("s (h d) -> (s h) d", d=HD),
                        in_offset=IndirectOffsetOnAxis(
                            ap=srcu[:, uc:uc + 1], axis=0),
                        bounds_check=_breg["r"], oob_is_err=False)
                O("gpsimd", f_g, reads=list(q_db) + [srcu_b],
                  writes=[qsel_b], dma=True)
            for uc in range(4):
                pt6, pt6_b = psb(5)

                def f_tr(eng, uc=uc, pt6=pt6):
                    return eng.transpose(pt6[0:64, 0:128], qsel[:, uc, :],
                                         idb[:])
                O("tensor", f_tr, reads=[qsel_b, idb_b], writes=[pt6_b])

                def f_ev(eng, uc=uc, pt6=pt6):
                    return eng.tensor_copy(
                        out=qst[0:64, uc * 128:(uc + 1) * 128],
                        in_=pt6[0:64, 0:128])
                O("vector", f_ev, reads=[pt6_b], writes=[qst_b])

            def f_st64(eng, h=h):
                return eng.tensor_scalar(out=qst[64:65, :],
                                         in0=ones1b[0:1, 0:U],
                                         scalar1=nbh[0:1, h:h + 1],
                                         scalar2=None, op0=OP.mult)
            O("vector", f_st64, reads=[ones1b_b, nbh_b], writes=[qst_b],
              small=True)

            for lc in range(16):
                p2, p2_b = psf(2 + lc % 2)

                def f_m(eng, lc=lc, p2=p2, kpa=kpa):
                    return eng.matmul(p2[:],
                                      kpa[:, lc * 128:(lc + 1) * 128],
                                      qst[:], start=True, stop=True)
                O("tensor", f_m, reads=[kpa_b, qst_b], writes=[p2_b])

                def f_e(eng, lc=lc, p2=p2):
                    return eng.activation(out=att[:, lc, :], in_=p2[:],
                                          func=AF.Exp)
                O("scalar", f_e, reads=[p2_b], writes=[att_b])
            for uc in range(4):
                p4, p4_b = psf(4)
                for lc in range(16):
                    def f_m(eng, uc=uc, lc=lc, p4=p4, h=h):
                        return eng.matmul(
                            p4[:, 0:72],
                            att[:, lc, uc * 128:(uc + 1) * 128],
                            v_sb[:, lc, h * 72:(h + 1) * 72],
                            start=(lc == 0), stop=(lc == 15))
                    O("tensor", f_m, reads=[att_b, v_b], writes=[p4_b])

                def f_rc(eng, p4=p4):
                    return eng.reciprocal(out=mxs[:, 0:1], in_=p4[:, 64:65])
                O("vector", f_rc, reads=[p4_b], writes=[mxs_b], small=True)
                TS(ostg[:], ostg_b, p4[:, 0:64], p4_b, mxs[:, 0:1],
                   OP.mult, extra=[mxs_b])

                def f_sc(eng, h=h, uc=uc):
                    return eng.indirect_dma_start(
                        out=of_d.rearrange("s (h d) -> (s h) d", d=HD),
                        out_offset=IndirectOffsetOnAxis(
                            ap=srcu[:, uc:uc + 1], axis=0),
                        in_=ostg[:], in_offset=None,
                        bounds_check=_breg["r"], oob_is_err=False)
                O("gpsimd", f_sc, reads=[ostg_b, srcu_b] + list(of_zb),
                  writes=[of_db[h]], dma=True)

        AR.reset(markA)
        off, off_b = AR.alloc([128, 8, SEQ], BF16, "offm")
        wo_s, wo_sb = AR.alloc([128, 8, D], BF16, "wo_s")
        xt_o, xt_ob = AR.alloc([128, D], F32, "oxt")

        def f_lwo(eng):
            return eng.dma_start(
                out=wo_s[:], in_=wot_i.rearrange("(kc p) n -> p kc n",
                                                 p=128))
        O("gpsimd", f_lwo, writes=[wo_sb], dma=True)
        for kc in range(8):
            def f_t(eng, kc=kc):
                return eng.dma_start(
                    out=off[:, kc, :],
                    in_=of_d[:, kc * 128:(kc + 1) * 128], transpose=True)
            O("sync", f_t, reads=list(of_db) + list(of_zb), writes=[off_b],
              dma=True)
        for mt in range(16):
            def f_lx(eng, mt=mt):
                return eng.dma_start(out=xt_o[:],
                                     in_=x_d[mt * 128:(mt + 1) * 128, :])
            O("sync", f_lx, reads=[x_db[mt]], writes=[xt_ob], dma=True)
            for nn in range(2):
                po, po_b = psf(nn)
                for kc in range(8):
                    def f_m(eng, kc=kc, mt=mt, nn=nn, po=po):
                        return eng.matmul(
                            po[:], off[:, kc, mt * 128:(mt + 1) * 128],
                            wo_s[:, kc, nn * 512:(nn + 1) * 512],
                            start=(kc == 0), stop=False)
                    O("tensor", f_m, reads=[off_b, wo_sb], writes=[po_b])

                def f_b(eng, nn=nn, po=po):
                    return eng.matmul(po[:], ones1b[0:1, 0:128],
                                      bo_s[:, nn * 512:(nn + 1) * 512],
                                      start=False, stop=True)
                O("tensor", f_b, reads=[ones1b_b, bo_b], writes=[po_b])

                def f_res(eng, nn=nn, po=po):
                    return eng.scalar_tensor_tensor(
                        out=xt_o[:, nn * 512:(nn + 1) * 512], in0=po[:],
                        scalar=1.0, in1=xt_o[:, nn * 512:(nn + 1) * 512],
                        op0=OP.mult, op1=OP.add)
                O("vector", f_res, reads=[po_b, xt_ob], writes=[xt_ob])

            def f_sx(eng, mt=mt):
                return eng.dma_start(out=x_d[mt * 128:(mt + 1) * 128, :],
                                     in_=xt_o[:])
            O("sync", f_sx, reads=[xt_ob], writes=[x_db[mt]], dma=True)

    AR.reset(0)
    for seq in range(SPC):
        emit_ffn(seq, first=True)
        emit_attn(seq)
        emit_ffn(seq, first=False)
        for ti in range(16):
            def f_out(eng, seq=seq, ti=ti):
                r0 = seq * SEQ + ti * 128
                return eng.dma_start(out=out_d[r0:r0 + 128, :],
                                     in_=x_d[ti * 128:(ti + 1) * 128, :])
            O("sync", f_out, reads=[x_db[ti]],
              writes=[out_db[seq * 16 + ti]], dma=True)

    O("vector", lambda e: e.nop(), reads=list(out_db))

    with nc.Block() as block:
        bld.finalize(nc, ctx, block)
    ctx.close()
    return nc


_NC_CACHE = None


def _get_nc():
    global _NC_CACHE
    if _NC_CACHE is None:
        _NC_CACHE = build_program()
    return _NC_CACHE


def kernel(L0_attn_x, L0_attn_Wq, L0_attn_bq, L0_attn_Wk, L0_attn_bk,
           L0_attn_Wv, L0_attn_bv, L0_attn_Wo, L0_attn_bo, L0_ffn_w_gate,
           L0_ffn_w_up, L0_ffn_w_down):
    x = np.ascontiguousarray(np.asarray(L0_attn_x, dtype=np.float32))
    Wq = np.asarray(L0_attn_Wq, np.float32)
    Wk = np.asarray(L0_attn_Wk, np.float32)
    Wv = np.asarray(L0_attn_Wv, np.float32)
    Wo = np.asarray(L0_attn_Wo, np.float32)
    bq = np.asarray(L0_attn_bq, np.float32)
    bk = np.asarray(L0_attn_bk, np.float32)
    bv = np.asarray(L0_attn_bv, np.float32)
    bo = np.asarray(L0_attn_bo, np.float32)
    wg = np.asarray(L0_ffn_w_gate, np.float32)
    wu = np.asarray(L0_ffn_w_up, np.float32)
    wd = np.asarray(L0_ffn_w_down, np.float32)

    pos2g = _gf_pos2g()
    valid = pos2g >= 0
    wg_p = np.zeros((3072, D), np.float32)
    wu_p = np.zeros((3072, D), np.float32)
    wg_p[valid] = wg[pos2g[valid]]
    wu_p[valid] = wu[pos2g[valid]]

    wvT = np.ascontiguousarray(Wv.T)
    wvtp = np.zeros((D, NH * 72), np.float32)
    bvp = np.zeros((1, NH * 72), np.float32)
    for h in range(NH):
        wvtp[:, h * 72:h * 72 + 64] = wvT[:, h * 64:(h + 1) * 64]
        bvp[0, h * 72:h * 72 + 64] = bv[h * 64:(h + 1) * 64]
        bvp[0, h * 72 + 64] = 1.0

    common = {
        "wqt": np.ascontiguousarray(Wq.T),
        "wkt": np.ascontiguousarray(Wk.T),
        "wvtp": wvtp,
        "wot": np.ascontiguousarray(Wo.T),
        "bq": bq.reshape(1, D),
        "bkr": bk.reshape(1, D),
        "bvp": bvp,
        "bo": bo.reshape(1, D),
        "wg": wg_p,
        "wu": wu_p,
        "wd": wd,
        "idb": np.eye(128, dtype=ml_dtypes.bfloat16),
        "idf": np.eye(128, dtype=np.float32),
        "iou": np.tile(np.arange(U, dtype=np.float32), (128, 1)),
    }
    nc = _get_nc()
    in_maps = []
    for c in range(NCORES):
        m = dict(common)
        m["x"] = np.ascontiguousarray(
            x[c * SPC:(c + 1) * SPC].reshape(SPC * SEQ, D))
        in_maps.append(m)
    res = run_bass_kernel_spmd(nc, in_maps, core_ids=list(range(NCORES)))
    outs = [res.results[c]["out"].reshape(SPC, SEQ, D)
            for c in range(NCORES)]
    return np.concatenate(outs, axis=0)
